# version 35
# baseline (speedup 1.0000x reference)
import numpy as np
import ml_dtypes
import concourse.bass as bass
import concourse.mybir as mybir
from concourse.bass_utils import run_bass_kernel_spmd
from contextlib import ExitStack

F32 = mybir.dt.float32
BF16 = mybir.dt.bfloat16
AF = mybir.ActivationFunctionType
ALU = mybir.AluOpType
AX = mybir.AxisListType

DEBUG_TAP = False
D = 1024
H = 16
NCH = 8
EPS = 1e-5
C0 = 10.0
C1 = 15.0
DFF = 4096
LD = 256
RANK = 256

bfnp = ml_dtypes.bfloat16


def slopes16():
    x = (2.0 ** 8) ** (1.0 / 16)
    return np.array([1.0 / x ** (i + 1) for i in range(16)], np.float64)


def l0_mstart(S):
    NT = S // 128
    out = []
    for sl in slopes16():
        w = int(np.ceil((36.0 / sl) / 128.0))
        out.append(NT - max(1, min(NT, w)))
    return out


class Ctr:
    __slots__ = ("h", "v", "name")

    def __init__(self, name):
        self.h = None
        self.v = 0
        self.name = name

    def bump(self, n=1):
        self.v += n
        return self.v


class Prog:
    def __init__(self):
        self.ops = {"sync": [], "gpsimd": [], "tensor": [], "vector": [], "scalar": []}

    def emit(self, eng, fn, kind="op"):
        self.ops[eng].append((fn, kind))

    def wait(self, eng, ctr, val):
        if val > 0:
            self.ops[eng].append(
                (lambda e, c=ctr, v=val: e.wait_ge(c.h, v), "wait"))

    def barrier(self, ctrs, engines=("tensor", "vector", "scalar")):
        vals = [(c, c.v) for c in ctrs]
        for e in engines:
            for c, v in vals:
                self.wait(e, c, v)


def build(S, nseq):
    NT = S // 128
    NSP = max(1, S // 512)
    SPW = min(512, S)
    MSTART = l0_mstart(S)

    nc = bass.Bass("TRN2", target_bir_lowering=False)

    dx = nc.dram_tensor("x", [nseq, D, S], F32, kind="ExternalInput").ap()
    dy = nc.dram_tensor("y", [nseq, D, S], F32, kind="ExternalOutput").ap()
    dscr = nc.dram_tensor("scr", [NCH, 128, S], F32).ap()
    ddbg = nc.dram_tensor("dbg", [3, 128, S], F32, kind="ExternalOutput").ap() if DEBUG_TAP else None

    def din(name, shape, dt=F32):
        return nc.dram_tensor(name, list(shape), dt, kind="ExternalInput").ap()

    dW = {
        "wq": din("wq", (128, NCH * 1024)), "wk": din("wk", (128, NCH * 1024)),
        "wv": din("wv", (128, NCH * 1024)), "wo": din("wo", (128, NCH * 1024)),
        "wt": din("wt", (128, NCH * 1024)),
        "ut": din("ut", (128, NCH * 128)), "vfl": din("vfl", (128, 1024)),
        "gwt": din("gwt", (128, NCH * 8)),
        "wq1": din("wq1", (128, NCH * 1024)),
        "down": din("down", (128, NCH * LD)),
        "kup": din("kup", (128, 2 * 1024)), "vup": din("vup", (128, 2 * 1024)),
        "wout": din("wout", (128, NCH * 1024)),
        "u1": din("u1", (128, NCH * RANK)), "v1": din("v1", (128, 2 * DFF)),
        "u2": din("u2", (128, 32 * RANK)), "v2": din("v2", (128, 2 * 1024)),
        "bqp": din("bqp", (128, 8)), "bkp": din("bkp", (128, 8)), "bop": din("bop", (128, 8)),
        "bvr": din("bvr", (1, 1024)), "b1r": din("b1r", (1, DFF)), "b2r": din("b2r", (1, 1024)),
        "cb0": din("cb0", (16, S), BF16),
        "calibi": din("calibi", (128, 16 * NT)),
        "cid": din("cid", (128, 128), BF16),
        "cmaskL": din("cmaskL", (128, 128), BF16), "cmaskR": din("cmaskR", (128, 128), BF16),
        "conesb": din("conesb", (128, 384), BF16),
        "crep": din("crep", (8, 128), BF16), "cones": din("cones", (1, 512), BF16),
        "conesd": din("conesd", (128, 128), BF16),
        "selF": din("selF", (128, 128), BF16), "selA": din("selA", (128, 128), BF16),
        "selB": din("selB", (128, 128), BF16),
    }

    pr = Prog()
    sT = Ctr("sT")
    sV = Ctr("sV")
    sS = Ctr("sS")
    sG = Ctr("sG")
    sWd = Ctr("sWd")
    sD = Ctr("sD")
    sP = Ctr("sP")
    CTRS = (sT, sV, sS, sG, sWd, sD, sP)

    with ExitStack() as stack:
        en = stack.enter_context
        R32 = en(nc.sbuf_tensor("R32", [128, NCH * S], F32))
        A16cols = 8 * S + NT * 1040 + 64 + max(4 * S, 3072 + DFF)
        H_in_qp = (8 * S >= 32 * 512)
        if not H_in_qp:
            H_off = A16cols
            A16cols += 32 * 512
        A16 = en(nc.sbuf_tensor("A16", [128, A16cols], BF16))
        WAr = en(nc.sbuf_tensor("WAr", [128, 20480], BF16))
        SM32 = en(nc.sbuf_tensor("SM32", [128, S], F32))
        SAL = en(nc.sbuf_tensor("SAL", [128, 16 * NT], F32))
        SBI = en(nc.sbuf_tensor("SBI", [128, 32], F32))
        CN = en(nc.sbuf_tensor("CN", [128, 2048], BF16))
        RW16 = en(nc.sbuf_tensor("RW16", [128, 2 * S], BF16))
        PSU = en(nc.psum_tensor("PSU", [128, 4096], F32))
        for c in CTRS:
            c.h = en(nc.semaphore(c.name))
        block = en(nc.Block())

        XN = 0
        VA = 8 * S
        AUG = 8 * S + NT * 1040 + 64

        R16 = R32.ap().bitcast(BF16)
        A16f = A16.ap().bitcast(F32)

        res_home = lambda c: R32[:, S * c: S * (c + 1)]
        res_vaug = lambda c: A16f[:, (VA // 2) + S * c: (VA // 2) + S * (c + 1)]
        qp = lambda c: R16[:, S * c: S * (c + 1)]
        kp = lambda c: R16[:, 8 * S + S * c: 8 * S + S * (c + 1)]
        ybuf = lambda c: R16[:, S * c: S * (c + 1)]
        xnc = lambda c: A16[:, XN + S * c: XN + S * (c + 1)]

        ptile = lambda i: A16[:, XN + S * i: XN + S * (i + 1)]
        nbc16 = lambda: A16[:, XN + 3 * S: XN + 4 * S]

        inv16 = RW16[0:1, 0:S]
        c16 = RW16[0:1, S: 2 * S]
        dinvr = (RW16[0:1, 0:S], RW16[32:33, 0:S])

        def vaug_lhs(mt, h):
            off = VA + 1040 * mt + 65 * h
            return A16[:, off: off + 128]

        def vaug_evac_out(mt):
            return A16[:, VA + 1040 * mt: VA + 1040 * mt + 1040].rearrange(
                "p (h c) -> p h c", c=65)[:, :, 0:64]

        vaug_ones_all = A16[:, VA: VA + 1040 * NT].rearrange(
            "p (h c) -> p h c", c=65)[:, :, 64:65]

        qaug = [A16[:, AUG + S * i: AUG + S * (i + 1)] for i in range(2)]
        kaug = [A16[:, AUG + 2 * S + S * i: AUG + 3 * S + S * i] for i in range(2)]
        aug_pad = [A16[64:128, AUG + S * i: AUG + S * (i + 1)] for i in range(4)]
        qaug_brow = [A16[64:65, AUG + S * i: AUG + S * (i + 1)] for i in range(2)]
        kaug_ones = A16[64:65, AUG + 2 * S: AUG + 4 * S]

        lnsl = lambda i: A16[:, AUG + S * i: AUG + S * (i + 1)]

        Pt = [A16[:, AUG + 512 * i: AUG + 512 * (i + 1)] for i in range(2)]
        Qt = [A16[:, AUG + 1024 + 512 * i: AUG + 1024 + 512 * (i + 1)] for i in range(2)]
        b1row = A16[0:1, AUG + 2048: AUG + 2048 + DFF]
        b2row = A16[0:1, AUG + 2048 + DFF: AUG + 3072 + DFF]
        bvrow = A16[0:1, AUG: AUG + 1024]

        g16 = lambda mt: R16[:, 3 * S + 8 * mt: 3 * S + 8 * mt + 8]
        gTrow = R16[0:8, 2 * S: 3 * S]
        T2b = R16[:, 0:S]
        grep16 = R16[:, S: 2 * S]
        latc = lambda c: R16[:, (5 + c) * S: (6 + c) * S]
        if H_in_qp:
            Hch = lambda hc: R16[:, 512 * hc: 512 * (hc + 1)]
        else:
            Hch = lambda hc: A16[:, H_off + 512 * hc: H_off + 512 * (hc + 1)]

        WA_ = 0
        WB_ = 8192
        SSo = 16384

        m2row = SM32[0:1, :]
        dsc = SM32[96:97, :]
        vepsrow = SM32[32:33, :]
        lnrow = SM32[64:65, :]
        gexp = lambda mt: SM32[:, 8 * mt: 8 * mt + 8]
        gsum = lambda mt: SM32[:, 128 + mt: 129 + mt]
        grin = lambda mt: SM32[:, 160 + mt: 161 + mt]

        ident = CN[:, 0:128]
        cmaskL = CN[:, 128:256]
        crep = CN[0:8, 256:384]
        cones = CN[0:1, 384:896]
        conesd = CN[:, 896:1024]
        selF = CN[:, 1024:1152]
        selA = CN[:, 1152:1280]
        selB = CN[:, 1280:1408]
        cmaskR = CN[:, 1408:1536]
        conesb = CN[:, 1536:1920]

        T = lambda fn: pr.emit("tensor", fn)

        def V(fn, chain=True):
            pr.emit("vector", fn, "op" if chain else "opnc")
            return sV.bump()

        def SC(fn, chain=True):
            pr.emit("scalar", fn, "op" if chain else "opnc")
            return sS.bump()

        def G(fn):
            pr.emit("gpsimd", fn)
            return sG.bump()

        SY = lambda fn: pr.emit("sync", fn)

        def d16(dst, src, wait=()):
            pr.wait("gpsimd", sWd, sWd.v)
            for (c, vv) in wait:
                pr.wait("gpsimd", c, vv)
            v = sWd.bump(16)
            pr.emit("gpsimd",
                    lambda g, d=dst, s=src: g.dma_start(out=d, in_=s).then_inc(sWd.h, 16),
                    kind="dma")
            return v

        def dsy(dst, src, wait=()):
            pr.wait("sync", sD, sD.v)
            for (c, vv) in wait:
                pr.wait("sync", c, vv)
            v = sD.bump(16)
            SY(lambda e, d=dst, s=src: e.dma_start(out=d, in_=s).then_inc(sD.h, 16))
            return v

        def dscq(dst, src, wait=()):
            pr.wait("scalar", sP, sP.v)
            for (c, vv) in wait:
                pr.wait("scalar", c, vv)
            v = sP.bump(16)
            pr.emit("scalar",
                    lambda e, d=dst, s=src: e.dma_start(out=d, in_=s).then_inc(sP.h, 16),
                    kind="dma")
            return v

        d16(ident, dW["cid"])
        d16(cmaskL, dW["cmaskL"])
        d16(cmaskR, dW["cmaskR"])
        d16(conesb, dW["conesb"])
        d16(crep, dW["crep"])
        d16(cones, dW["cones"])
        d16(conesd, dW["conesd"])
        d16(selF, dW["selF"])
        d16(selA, dW["selA"])
        consts_w = d16(selB, dW["selB"])
        G(lambda g: g.memset(RW16[:, :], 0.0))
        G(lambda g: g.memset(A16[:, AUG - 64: AUG], 0.0))
        dsy(SAL[:, :], dW["calibi"])
        dsy(SBI[:, 0:8], dW["bqp"])
        dsy(SBI[:, 8:16], dW["bkp"])
        bias_d = dsy(SBI[:, 16:24], dW["bop"])
        pr.wait("scalar", sD, bias_d)
        SC(lambda e: e.activation(SBI[:, 24:32], SBI[:, 0:8], AF.Copy, scale=0.125))

        slot_free = {"A": (sT, 0), "B": (sT, 0), "SS": (sT, 0)}

        def wload(slot, dname, cols, sub=0):
            base = {"A": WA_, "B": WB_, "SS": SSo}[slot]
            ctr, val = slot_free[slot]
            dst = WAr[:, base + sub: base + sub + cols]
            return d16(dst, dW[dname], wait=((ctr, val),))

        def wsl(base):
            return lambda k, c: WAr[:, base + 1024 * k + 128 * c: base + 1024 * k + 128 * c + 128]

        def ln_phase(res_ap, ready=None, psum_free=None, xn_safe=True):
            if ready is None:
                pr.barrier(CTRS)
            mm_done = [0] * NCH
            for c in range(NCH):
                x16 = xnc(c) if xn_safe else lnsl(c % 2)
                xsq = lnsl(2 + c % 2)
                if ready is not None:
                    for ctr, val in ready(c):
                        pr.wait("vector", ctr, val)
                        pr.wait("scalar", ctr, val)
                if c >= 2:
                    pr.wait("vector", sT, mm_done[c - 2])
                    pr.wait("scalar", sT, mm_done[c - 2])
                gvv = SC(lambda e, o=x16, i=res_ap(c): e.activation(o, i, AF.Copy),
                         chain=False)
                svv = SC(lambda e, o=xsq, i=res_ap(c): e.activation(o, i, AF.Square),
                         chain=False)
                pr.wait("tensor", sS, gvv)
                pr.wait("tensor", sS, svv)
                if c == 0:
                    pr.wait("tensor", sWd, consts_w)
                    if psum_free is not None:
                        pr.wait("tensor", sV, psum_free[0])
                        pr.wait("tensor", sV, psum_free[1])
                for j in range(NSP):
                    T(lambda t, o=PSU[:, j * SPW:(j + 1) * SPW], r=x16[:, j * SPW:(j + 1) * SPW],
                      st=(c == 0), sp=(c == NCH - 1): t.matmul(o, conesd, r, start=st, stop=sp))
                for j in range(NSP):
                    fin = (j == NSP - 1)
                    if fin:
                        mm_done[c] = sT.bump()
                    def _mm(t, o=PSU[:, 2048 + j * SPW: 2048 + (j + 1) * SPW],
                            r=xsq[:, j * SPW:(j + 1) * SPW], st=(c == 0),
                            sp=(c == NCH - 1), fin=fin):
                        m = t.matmul(o, conesd, r, start=st, stop=sp)
                        return m.then_inc(sT.h, 1) if fin else m
                    T(_mm)
            pr.wait("scalar", sT, mm_done[NCH - 1])
            s_m2 = SC(lambda e: e.activation(m2row, PSU[0:1, 0:S], AF.Square))
            pr.wait("vector", sS, s_m2)
            v_veps = V(lambda e: e.scalar_tensor_tensor(vepsrow, PSU[0:1, 2048:2048 + S], EPS, m2row,
                                               ALU.add, ALU.subtract))
            pr.wait("scalar", sV, v_veps)
            s_ln = SC(lambda e: e.activation(lnrow, vepsrow, AF.Ln))
            pr.wait("scalar", sS, s_ln)
            s_inv = SC(lambda e: e.activation(inv16, lnrow, AF.Exp, scale=-0.5))
            pr.wait("vector", sS, s_inv)
            v_c16 = V(lambda e: e.scalar_tensor_tensor(c16, PSU[0:1, 0:S], -1.0, inv16,
                                               ALU.mult, ALU.mult))
            pr.wait("tensor", sS, s_inv)
            ab_done = 0
            for j in range(NSP):
                fin = (j == NSP - 1)
                if fin:
                    ab_done = sT.bump()
                def _mm(t, o=PSU[:, 2048 + j * SPW: 2048 + (j + 1) * SPW],
                        r=RW16[:, j * SPW:(j + 1) * SPW], fin=fin):
                    m = t.matmul(o, selF, r, start=True, stop=True)
                    return m.then_inc(sT.h, 1) if fin else m
                T(_mm)
            pr.wait("tensor", sV, v_c16)
            cb_done = 0
            for j in range(NSP):
                fin = (j == NSP - 1)
                if fin:
                    cb_done = sT.bump()
                def _mm(t, o=PSU[:, j * SPW:(j + 1) * SPW],
                        r=RW16[:, S + j * SPW: S + (j + 1) * SPW], fin=fin):
                    m = t.matmul(o, selF, r, start=True, stop=True)
                    return m.then_inc(sT.h, 1) if fin else m
                T(_mm)
            ab16 = lnsl(0)
            cb16 = lnsl(1)
            pr.wait("vector", sT, ab_done)
            V(lambda e: e.tensor_copy(ab16, PSU[:, 2048:2048 + S]), chain=False)
            for c in range(NCH):
                if xn_safe:
                    V(lambda e, o=xnc(c): e.tensor_tensor(o, o, ab16, ALU.mult),
                      chain=(c == 0))
                else:
                    V(lambda e, o=xnc(c), i=res_ap(c): e.tensor_tensor(o, i, ab16, ALU.mult),
                      chain=(c == 0))
                if c == 0:
                    pr.wait("vector", sT, cb_done)
                    V(lambda e: e.tensor_copy(cb16, PSU[:, 0:S]), chain=False)
            out = 0
            for c in range(NCH):
                out = V(lambda e, o=xnc(c): e.tensor_tensor(o, o, cb16, ALU.add),
                        chain=False)
            return out

        def proj(dst_fn, w_ap, w_loaded, nk, rhs_fn, evac, bias_aug=None,
                 prev_half_free=(0, 0), ln_wait=0, ev_ctr=None):
            if ev_ctr is None:
                ev_ctr = sS
            lastmm = [0] * NCH
            ev = [0] * NCH
            pr.wait("tensor", sWd, w_loaded)
            if ln_wait:
                pr.wait("tensor", sV, ln_wait)
            for c in range(NCH):
                half = 2048 * (c % 2)
                if c >= 2:
                    pr.wait("tensor", ev_ctr, ev[c - 2])
                elif prev_half_free[c % 2]:
                    pr.wait("tensor", ev_ctr, prev_half_free[c % 2])
                for j in range(NSP):
                    last_pass = (bias_aug is None)
                    for k in range(nk):
                        fin = (j == NSP - 1 and k == nk - 1 and last_pass)
                        if fin:
                            lastmm[c] = sT.bump()
                        def _mm(t, o=PSU[:, half + j * SPW: half + (j + 1) * SPW],
                                l=w_ap(k, c), r=rhs_fn(k, j), st=(k == 0),
                                sp=(k == nk - 1 and last_pass), fin=fin):
                            m = t.matmul(o, l, r, start=st, stop=sp)
                            return m.then_inc(sT.h, 1) if fin else m
                        T(_mm)
                    if bias_aug is not None:
                        lrow, rrow = bias_aug
                        fin = (j == NSP - 1)
                        if fin:
                            lastmm[c] = sT.bump()
                        def _mm(t, o=PSU[:, half + j * SPW: half + (j + 1) * SPW],
                                l=lrow(c), r=rrow(j), fin=fin):
                            m = t.matmul(o, l, r, start=False, stop=True)
                            return m.then_inc(sT.h, 1) if fin else m
                        T(_mm)
                pr.wait("scalar" if ev_ctr is sS else "vector", sT, lastmm[c])
                ev[c] = evac(c, PSU[:, 2048 * (c % 2): 2048 * (c % 2) + S])
            return lastmm, ev

        def emit_pads(layer):
            pr.wait("vector", sD, sD.v)
            pr.wait("vector", sS, sS.v)
            for i in range(4):
                V(lambda e, o=aug_pad[i]: e.memset(o, 0.0), chain=False)
            if layer == 1:
                V(lambda e: e.memset(qaug_brow[0], -C1), chain=False)
                V(lambda e: e.memset(qaug_brow[1], -C1), chain=False)
            return V(lambda e: e.memset(kaug_ones, 1.0), chain=False)

        def head_loop(layer, kept_fn, act_fn, bias_fn, pads_vv):
            state = {"y_free": 0, "ynorm_last": 0}
            pend_norm = [None]
            pend_copies = [None]
            dinv_done = [(sV, 0), (sV, 0)]
            ptile_last = [0, 0, 0]
            last_score_prev_pair = 0
            prev_exp_half = [0, 0]
            last_pv = 0
            ynorm_last = 0

            def emit_pv(h_, gseg, exp_v, zv_, pbuf_, first_group):
                if first_group and pend_norm[0] is not None:
                    state["ynorm_last"] = pend_norm[0]()
                    pend_norm[0] = None
                pr.wait("tensor", sS, exp_v)
                if zv_:
                    pr.wait("tensor", sV, zv_)
                if first_group and state["y_free"]:
                    pr.wait("tensor", sV, state["y_free"])
                pvv = 0
                kept0_ = kept_fn(h_)[0]
                keptN_ = kept_fn(h_)[-1]
                for si, (mc, j, pc) in enumerate(gseg):
                    fin = (si == len(gseg) - 1)
                    if fin:
                        pvv = sT.bump()
                    if layer == 1:
                        st = (mc == 0)
                        sp = (mc == min(4 * j + 3, NT - 1))
                    else:
                        st = (mc == kept0_)
                        sp = (mc == keptN_)
                    def _mm(t, o=PSU[:, 2048 + j * SPW: 2048 + (j + 1) * SPW],
                            l=vaug_lhs(mc, h_), rr=pbuf_[:, pc:pc + SPW],
                            st=st, sp=sp, fin=fin):
                        m = t.matmul(o, l, rr, start=st, stop=sp)
                        return m.then_inc(sT.h, 1) if fin else m
                    T(_mm)
                return pvv

            def emit_copies(p_, score_wait):
                pr.wait("vector", sT, score_wait)
                qsrc = qp(p_)
                ksrc = kp(p_)
                V(lambda e, o=qaug[0][0:64, :], i=qsrc[0:64, :]: e.tensor_copy(o, i), chain=False)
                V(lambda e, o=qaug[1][0:64, :], i=qsrc[64:128, :]: e.tensor_copy(o, i), chain=False)
                V(lambda e, o=kaug[0][0:64, :], i=ksrc[0:64, :]: e.tensor_copy(o, i), chain=False)
                if layer == 1:
                    cv_ = V(lambda e, o=kaug[1][0:64, :], i=ksrc[64:128, :]: e.tensor_copy(o, i),
                            chain=False)
                    bload_ = 0
                else:
                    cv_ = V(lambda e, o=kaug[1][0:64, :], i=ksrc[64:128, :]: e.tensor_copy(o, i),
                            chain=False)
                    dsy(qaug_brow[0], dW["cb0"][2 * p_:2 * p_ + 1, :],
                        wait=((sT, score_wait), (sV, pads_v)))
                    bload_ = dsy(qaug_brow[1], dW["cb0"][2 * p_ + 1:2 * p_ + 2, :])
                return cv_, bload_

            pads_v = pads_vv
            for p in range(NCH):
                if pend_copies[0] is not None:
                    cv, bload = pend_copies[0]
                    pend_copies[0] = None
                else:
                    cv, bload = emit_copies(p, last_score_prev_pair)
                yev = 0
                for hh in range(2):
                    h = 2 * p + hh
                    kept = kept_fn(h)
                    pend_pv = []
                    sc_v = 0
                    groups = []
                    if layer == 0:
                        for mc in kept:
                            groups.append([(mc, j, j * SPW) for j in range(NSP)])
                    else:
                        mc = 0
                        while mc < NT:
                            j0 = (128 * mc) // SPW
                            width = (NSP - j0) * SPW
                            nbatch = max(1, (NSP * SPW) // width) if width <= 1024 else 1
                            gseg = []
                            base = 0
                            for bmc in range(mc, min(NT, mc + nbatch)):
                                for j in range(j0, NSP):
                                    gseg.append((bmc, j, base))
                                    base += SPW
                            groups.append(gseg)
                            mc = min(NT, mc + nbatch)
                    for idx, gseg in enumerate(groups):
                        pbuf = ptile(idx % 3)
                        if idx == 0:
                            pr.wait("tensor", sV, cv)
                            if bload:
                                pr.wait("tensor", sD, bload)
                        lo = min(pc for _, _, pc in gseg)
                        hi = max(pc for _, _, pc in gseg) + SPW
                        mid = min(2 * SPW, S)
                        halves = []
                        if lo < mid:
                            halves.append((0, lo, min(mid, hi)))
                        if hi > mid:
                            halves.append((1, max(lo, mid), hi))
                        ptl = ptile_last[idx % 3]
                        ev_ = 0
                        gzv = 0
                        for hx, (hid, a0, b0) in enumerate(halves):
                            pr.wait("tensor", sS, prev_exp_half[hid])
                            segs = [sg for sg in gseg if a0 <= sg[2] < b0]
                            exp_lo = a0
                            for si, (mc, j, pc) in enumerate(segs):
                                j0c = (128 * mc) // SPW if layer == 1 else 0
                                r = mc - 4 * j0c if layer == 1 else 0
                                diag = (layer == 1 and j == j0c)
                                last_seg = (si == len(segs) - 1)
                                fin = last_seg and not diag
                                if fin:
                                    sc_v = sT.bump()
                                def _mm(t, o=PSU[:, pc:pc + SPW],
                                        l=kaug[hh][:, 128 * mc:128 * mc + 128],
                                        rr=qaug[hh][:, j * SPW:(j + 1) * SPW],
                                        fin=fin, sp=not diag):
                                    m = t.matmul(o, l, rr, start=True, stop=sp)
                                    return m.then_inc(sT.h, 1) if fin else m
                                T(_mm)
                                if diag:
                                    if r > 0:
                                        T(lambda t, o=PSU[:, pc: pc + 128 * r],
                                          rr=conesb[:, 0:128 * r]:
                                          t.matmul(o, cmaskL, rr, start=False, stop=False))
                                    fin2 = last_seg
                                    if fin2:
                                        sc_v = sT.bump()
                                    def _mm2(t, o=PSU[:, pc + 128 * r: pc + 128 * r + 128], fin=fin2):
                                        m = t.matmul(o, cmaskL, cmaskR, start=False, stop=True)
                                        return m.then_inc(sT.h, 1) if fin else m
                                    T(_mm2)
                            pr.wait("scalar", sT, sc_v)
                            if ptl:
                                pr.wait("scalar", sT, ptl)
                            bias = bias_fn(h, gseg[0][0])
                            if bias is None:
                                ev_ = SC(lambda e, o=pbuf[:, exp_lo:b0], i=PSU[:, exp_lo:b0]:
                                         e.activation(o, i, AF.Exp), chain=False)
                            else:
                                ev_ = SC(lambda e, o=pbuf[:, exp_lo:b0], i=PSU[:, exp_lo:b0], b=bias:
                                         e.activation(o, i, AF.Exp, bias=b), chain=False)
                            prev_exp_half[hid] = ev_
                        pend_pv.append((idx, gseg, ev_, gzv, pbuf, idx == 0))
                        if len(pend_pv) > 2:
                            i_, *args = pend_pv.pop(0)
                            last_pv = emit_pv(h, *args)
                            ptile_last[i_ % 3] = last_pv
                    while pend_pv:
                        i_, *args = pend_pv.pop(0)
                        last_pv = emit_pv(h, *args)
                        ptile_last[i_ % 3] = last_pv
                    last_score_prev_pair = sc_v
                    if hh == 1 and p + 1 < NCH:
                        pend_copies[0] = emit_copies(p + 1, sc_v)
                    if hh == 0:
                        pr.wait("vector", sT, last_pv)
                        V(lambda e: e.tensor_copy(dsc, PSU[64:65, 2048:2048 + S]),
                          chain=False)
                    else:
                        pr.wait("scalar", sT, last_pv)
                        SC(lambda e: e.activation(m2row, PSU[64:65, 2048:2048 + S], AF.Ln))
                        def _dinv(e, o=dinvr[hh]):
                            with nc.allow_low_precision(reason="bf16 dinv is matmul-operand rounding"):
                                return e.activation(o, m2row, AF.Exp, scale=-1.0)
                        dnv = SC(_dinv)
                        dinv_done[hh] = (sS, dnv)
                        pr.wait("vector", sT, last_pv)
                    yout = ybuf(p)[0:64, :] if hh == 0 else ybuf(p)[64:128, :]
                    yev = V(lambda e, o=yout: e.tensor_copy(o, PSU[0:64, 2048:2048 + S]),
                            chain=False)
                    state["y_free"] = yev
                    if hh == 0:
                        def _dinv(e, o=dinvr[hh]):
                            with nc.allow_low_precision(reason="bf16 dinv is matmul-operand rounding"):
                                return e.reciprocal(o, dsc)
                        dnv = V(_dinv)
                        dinv_done[hh] = (sV, dnv)
                def _norm(p_=p, yev_=yev, dd=dinv_done[1], da=dinv_done[0]):
                    pr.wait("tensor", sV, yev_)
                    pr.wait("tensor", da[0], da[1])
                    pr.wait("tensor", dd[0], dd[1])
                    bcv = 0
                    for j in range(NSP):
                        fin = (j == NSP - 1)
                        if fin:
                            bcv = sT.bump()
                        def _mm(t, o=PSU[:, 2048 + j * SPW: 2048 + (j + 1) * SPW],
                                r=RW16[:, j * SPW:(j + 1) * SPW], fin=fin):
                            m = t.matmul(o, selA, r, start=True, stop=True)
                            return m.then_inc(sT.h, 1) if fin else m
                        T(_mm)
                    pr.wait("vector", sT, bcv)
                    cpv = V(lambda e: e.tensor_copy(nbc16(), PSU[:, 2048:2048 + S]),
                            chain=False)
                    nv = V(lambda e, o=ybuf(p_): e.tensor_tensor(o, o, nbc16(), ALU.mult))
                    state["y_free"] = cpv
                    return nv
                pend_norm[0] = _norm
            state["ynorm_last"] = pend_norm[0]()
            pend_norm[0] = None
            return last_pv, state["ynorm_last"], last_score_prev_pair

        def out_proj(w_ap, w_loaded, res_ap, bias_col, reload_marks, ynorm_last):
            pr.wait("tensor", sWd, w_loaded)
            pr.wait("tensor", sV, ynorm_last)
            ep = [0] * NCH
            for oc in range(NCH):
                half = 2048 * (oc % 2)
                if oc >= 2:
                    pr.wait("tensor", sV, ep[oc - 2])
                mmv = 0
                for j in range(NSP):
                    for k in range(NCH):
                        fin = (j == NSP - 1 and k == NCH - 1)
                        if fin:
                            mmv = sT.bump()
                        def _mm(t, o=PSU[:, half + j * SPW: half + (j + 1) * SPW],
                                l=w_ap(k, oc), r=ybuf(k)[:, j * SPW:(j + 1) * SPW],
                                st=(k == 0), sp=(k == NCH - 1), fin=fin):
                            m = t.matmul(o, l, r, start=st, stop=sp)
                            return m.then_inc(sT.h, 1) if fin else m
                        T(_mm)
                pr.wait("vector", sT, mmv)
                for (c_, v_) in reload_marks:
                    pr.wait("vector", c_, v_)
                if bias_col is not None:
                    ep[oc] = V(lambda e, o=res_ap(oc), i=PSU[:, half: half + S], b=bias_col(oc):
                               e.scalar_tensor_tensor(o, i, b, o, ALU.add, ALU.add),
                               chain=False)
                else:
                    ep[oc] = V(lambda e, o=res_ap(oc), i=PSU[:, half: half + S]:
                               e.tensor_tensor(o, o, i, ALU.add), chain=False)
            return ep

        prev_ystoreA = 0
        prev_allT = 0
        prev_hch_free = 0
        prev_psum_free = [0, 0]
        dx0r = dx[0].rearrange("(c p) t -> p c t", p=128)
        a1 = dsy(R32[:, 0:2 * S].rearrange("p (c t) -> p c t", t=S), dx0r[:, 0:2, :])
        a2 = d16(R32[:, 2 * S:4 * S].rearrange("p (c t) -> p c t", t=S), dx0r[:, 2:4, :])
        xin_map = {0: (sD, a1), 1: (sD, a1), 2: (sWd, a2), 3: (sWd, a2)}
        for ci in range(4):
            dq = dsy if ci % 2 == 0 else d16
            v = dq(R32[:, (4 + ci) * S:(5 + ci) * S], dx0r[:, 4 + ci, :])
            xin_map[4 + ci] = ((sD if ci % 2 == 0 else sWd), v)
        for s in range(nseq):
            dxr = dx[s].rearrange("(c p) t -> p c t", p=128)

            ln1 = ln_phase(res_home,
                           ready=lambda c, m=dict(xin_map), pa=prev_allT:
                               [m[c], (sT, pa)],
                           psum_free=tuple(prev_psum_free))

            wq_l = wload("A", "wq", 8192)
            wk_l = wload("B", "wk", 8192)
            xn_rhs = lambda k, j: xnc(k)[:, j * SPW:(j + 1) * SPW]

            def ev_q(c, ps):
                vv = SC(lambda e, o=qp(c), i=ps, b=SBI[:, 24 + c:25 + c]:
                  e.activation(o, i, AF.Identity, bias=b, scale=0.125), chain=False)
                return vv

            ln1_T = sT.v
            pads0 = emit_pads(0)
            qmm, qev = proj(qp, wsl(WA_), wq_l, NCH, xn_rhs, ev_q, ln_wait=ln1)
            slot_free["A"] = (sT, qmm[-1])
            wv_l = wload("A", "wv", 8192)
            bvr_l = d16(bvrow, dW["bvr"], wait=((sT, ln1_T),))
            if prev_ystoreA:
                pr.emit("gpsimd", lambda g, v=prev_ystoreA[0]: g.wait_ge(sD.h, v), kind="wait")
            gva = G(lambda g: g.memset(vaug_ones_all, 1.0))

            def ev_k(c, ps):
                vv = SC(lambda e, o=kp(c), i=ps, b=SBI[:, 8 + c:9 + c]:
                  e.activation(o, i, AF.Identity, bias=b), chain=False)
                return vv

            kmm, kev = proj(kp, wsl(WB_), wk_l, NCH, xn_rhs, ev_k,
                            prev_half_free=(qev[6], qev[7]))
            slot_free["B"] = (sT, kmm[-1])
            wo_l = wload("B", "wo", 8192)

            pr.wait("tensor", sWd, wv_l)
            pr.wait("tensor", sWd, bvr_l)
            vev = {}
            vmm_last = 0
            for mt in range(NT):
                bank = (mt % 4) * 1024
                if mt >= 4:
                    pr.wait("tensor", sV, vev[mt - 4])
                elif mt == 0:
                    pr.wait("tensor", sS, kev[6])
                elif mt == 2:
                    pr.wait("tensor", sS, kev[7])
                for nh in range(2):
                    for k in range(NCH):
                        T(lambda t, o=PSU[:, bank + 512 * nh: bank + 512 * nh + 512],
                          l=xnc(k)[:, 128 * mt: 128 * mt + 128],
                          r=WAr[:, WA_ + 1024 * k + 512 * nh: WA_ + 1024 * k + 512 * nh + 512],
                          st=(k == 0): t.matmul(o, l, r, start=st, stop=False))
                    vmm_last = sT.bump()
                    T(lambda t, o=PSU[:, bank + 512 * nh: bank + 512 * nh + 512],
                      r=bvrow[:, 512 * nh: 512 * nh + 512]:
                      t.matmul(o, cones[0:1, 0:128], r, start=False, stop=True).then_inc(sT.h, 1))
                pr.wait("vector", sT, vmm_last)
                vv = V(lambda e, o=vaug_evac_out(mt), i=PSU[:, bank: bank + 1024].rearrange(
                    "p (h c) -> p h c", c=64): e.tensor_copy(o, i), chain=False)
                vev[mt] = vv
            slot_free["A"] = (sT, vmm_last)
            wt_l = wload("A", "wt", 8192)

            pr.barrier((sT, sV, sS, sG))
            kept0 = lambda h: list(range(MSTART[h], NT))
            act0 = lambda mc: (0, 0)
            ab0 = lambda h, mc: SAL[:, NT * h + mc: NT * h + mc + 1]
            lpv0, yn0, lsc0 = head_loop(0, kept0, act0, ab0, pads0)

            rl_a = dsy(A16f[:, VA // 2: VA // 2 + 4 * S].rearrange("p (c t) -> p c t", t=S),
                       dxr[:, 0:4, :], wait=((sT, lpv0),))
            rl_b = dsy(R32[:, 4 * S: 8 * S].rearrange("p (c t) -> p c t", t=S),
                       dxr[:, 4:8, :])

            res_mixed = lambda c: res_vaug(c) if c < 4 else res_home(c)

            ep0 = out_proj(wsl(WB_), wo_l, res_mixed,
                           lambda oc: SBI[:, 16 + oc: 17 + oc],
                           ((sD, rl_a), (sD, rl_b)), yn0)
            slot_free["B"] = (sT, sT.v)

            ut_l = wload("SS", "ut", 1024, sub=0)
            vfl_l = wload("SS", "vfl", 1024, sub=1024)
            gwt_l = wload("SS", "gwt", 64, sub=2048)
            ln2 = ln_phase(res_mixed, ready=lambda c: [(sV, ep0[c])],
                           psum_free=(ep0[6], ep0[7]))

            pr.wait("tensor", sWd, gwt_l)
            pr.wait("tensor", sV, ln2)
            gate_mm = {}
            for mt in range(NT):
                for k in range(NCH):
                    fin = (k == NCH - 1)
                    if fin:
                        gate_mm[mt] = sT.bump()
                    def _mm(t, o=PSU[:, 8 * mt: 8 * mt + 8],
                            l=xnc(k)[:, 128 * mt: 128 * mt + 128],
                            r=WAr[:, SSo + 2048 + 8 * k: SSo + 2048 + 8 * k + 8],
                            st=(k == 0), sp=fin, fin=fin):
                        m = t.matmul(o, l, r, start=st, stop=sp)
                        return m.then_inc(sT.h, 1) if fin else m
                    T(_mm)
            pr.wait("scalar", sT, gate_mm[NT - 1])
            gex_all = SC(lambda e: e.activation(SM32[:, 0:8 * NT], PSU[:, 0:8 * NT], AF.Exp))
            gex = {mt: gex_all for mt in range(NT)}
            g_done = {}
            for mt in range(NT):
                pr.wait("vector", sS, gex[mt])
                V(lambda e, o=gsum(mt), i=gexp(mt): e.reduce_sum(o, i, AX.X))
                V(lambda e, o=grin(mt), i=gsum(mt): e.reciprocal(o, i))
                g_done[mt] = V(lambda e, o=g16(mt), i=gexp(mt), r=grin(mt):
                  e.tensor_scalar(o, i, r, None, ALU.mult))
            gT_ev = {}
            psbf = PSU[:, 2048:3072].bitcast(BF16)
            for mt in range(NT):
                pr.wait("tensor", sV, g_done[mt])
                tv = sT.bump()
                T(lambda t, o=psbf[0:8, 128 * mt: 128 * mt + 128], i=g16(mt):
                  t.transpose(o, i, ident).then_inc(sT.h, 1))
                pr.wait("vector", sT, tv)
                gT_ev[mt] = V(lambda e, o=gTrow[:, 128 * mt: 128 * mt + 128],
                  i=psbf[0:8, 128 * mt: 128 * mt + 128]:
                  e.tensor_copy(o, i))
            pr.wait("tensor", sV, gT_ev[NT - 1])
            pr.wait("tensor", sS, gex[NT - 1])
            grep_mm = 0
            for j in range(NSP):
                fin = (j == NSP - 1)
                if fin:
                    grep_mm = sT.bump()
                def _mm(t, o=PSU[:, j * SPW:(j + 1) * SPW], r=gTrow[:, j * SPW:(j + 1) * SPW], fin=fin):
                    m = t.matmul(o, crep, r, start=True, stop=True)
                    return m.then_inc(sT.h, 1) if fin else m
                T(_mm)
            pr.wait("tensor", sWd, ut_l)
            t_mm = 0
            for j in range(NSP):
                for k in range(NCH):
                    fin = (j == NSP - 1 and k == NCH - 1)
                    if fin:
                        t_mm = sT.bump()
                    def _mm(t, o=PSU[:, 2048 + j * SPW: 2048 + (j + 1) * SPW],
                            l=WAr[:, SSo + 128 * k: SSo + 128 * k + 128],
                            r=xnc(k)[:, j * SPW:(j + 1) * SPW], st=(k == 0),
                            sp=(k == NCH - 1), fin=fin):
                        m = t.matmul(o, l, r, start=st, stop=sp)
                        return m.then_inc(sT.h, 1) if fin else m
                    T(_mm)
            pr.wait("vector", sT, grep_mm)
            V(lambda e: e.tensor_copy(grep16, PSU[:, 0:S]))
            pr.wait("vector", sT, t_mm)
            v_t2 = V(lambda e: e.tensor_tensor(T2b, PSU[:, 2048:2048 + S], grep16, ALU.mult))

            pr.wait("tensor", sWd, wt_l)
            pr.wait("tensor", sWd, vfl_l)
            pr.wait("tensor", sV, v_t2)
            ep1 = [0] * NCH
            for oc in range(NCH):
                half = 2048 * (oc % 2)
                if oc >= 2:
                    pr.wait("tensor", sV, ep1[oc - 2])
                fin_l = 0
                for j in range(NSP):
                    for k in range(NCH):
                        T(lambda t, o=PSU[:, half + j * SPW: half + (j + 1) * SPW],
                          l=WAr[:, WA_ + 1024 * k + 128 * oc: WA_ + 1024 * k + 128 * oc + 128],
                          r=xn_rhs(k, j), st=(k == 0): t.matmul(o, l, r, start=st, stop=False))
                    fin_l = sT.bump()
                    T(lambda t, o=PSU[:, half + j * SPW: half + (j + 1) * SPW],
                      l=WAr[:, SSo + 1024 + 128 * oc: SSo + 1024 + 128 * oc + 128],
                      r=T2b[:, j * SPW:(j + 1) * SPW]:
                      t.matmul(o, l, r, start=False, stop=True).then_inc(sT.h, 1))
                pr.wait("vector", sT, fin_l)
                ep1[oc] = V(lambda e, o=res_mixed(oc), i=PSU[:, half: half + S]:
                  e.tensor_tensor(o, o, i, ALU.add), chain=False)
            slot_free["A"] = (sT, sT.v)
            slot_free["SS"] = (sT, sT.v)

            wq1_l = wload("A", "wq1", 8192)
            down_l = wload("SS", "down", 2048, sub=0)
            kup_l = wload("SS", "kup", 2048, sub=2048)
            wout_l = wload("B", "wout", 8192)
            sp2a = dsy(dscr[0:4].rearrange("c p t -> p c t"),
                       A16f[:, VA // 2: VA // 2 + 4 * S].rearrange("p (c t) -> p c t", t=S),
                       wait=((sV, ep1[3]),))
            sp2b = dsy(dscr[4:8].rearrange("c p t -> p c t"),
                       R32[:, 4 * S: 8 * S].rearrange("p (c t) -> p c t", t=S),
                       wait=((sV, ep1[7]),))
            ln3 = ln_phase(res_mixed, ready=lambda c: [(sV, ep1[c])],
                           psum_free=(ep1[6], ep1[7]), xn_safe=False)
            pr.emit("gpsimd", lambda g, v=sp2a: g.wait_ge(sD.h, v), kind="wait")
            pr.emit("gpsimd", lambda g, v=lpv0: g.wait_ge(sT.h, v), kind="wait")
            pr.emit("gpsimd", lambda g, v=ln3: g.wait_ge(sV.h, v), kind="wait")
            G(lambda g: g.memset(vaug_ones_all, 1.0))

            pads1 = emit_pads(1)
            pr.wait("tensor", sWd, down_l)
            pr.wait("tensor", sV, ln3)
            lat_ev = [0, 0]
            for c in range(2):
                half = 2048 * (c % 2)
                lmm = 0
                for j in range(NSP):
                    for k in range(NCH):
                        fin = (j == NSP - 1 and k == NCH - 1)
                        if fin:
                            lmm = sT.bump()
                        def _mm(t, o=PSU[:, half + j * SPW: half + (j + 1) * SPW],
                                l=WAr[:, SSo + LD * k + 128 * c: SSo + LD * k + 128 * c + 128],
                                r=xn_rhs(k, j), st=(k == 0), sp=(k == NCH - 1), fin=fin):
                            m = t.matmul(o, l, r, start=st, stop=sp)
                            return m.then_inc(sT.h, 1) if fin else m
                        T(_mm)
                pr.wait("scalar", sT, lmm)
                lat_ev[c] = SC(lambda e, o=latc(c), i=PSU[:, half: half + S]:
                  e.activation(o, i, AF.Copy), chain=False)
            vup_l = d16(WAr[:, SSo: SSo + 2048], dW["vup"], wait=((sT, lmm),))
            pr.wait("tensor", sWd, kup_l)
            pr.wait("tensor", sS, lat_ev[1])
            lat_rhs = lambda k, j: latc(k)[:, j * SPW:(j + 1) * SPW]

            def ev_kp1(c, ps):
                pr.wait("scalar", sD, sp2b)
                vv = SC(lambda e, o=kp(c), i=ps: e.activation(o, i, AF.Copy), chain=False)
                return vv

            kmm1, kev1 = proj(kp, lambda k, c: WAr[:, SSo + 2048 + 1024 * k + 128 * c:
                                                   SSo + 2048 + 1024 * k + 128 * c + 128],
                              kup_l, 2, lat_rhs, ev_kp1)
            pr.wait("tensor", sWd, vup_l)
            pr.wait("tensor", sS, kev1[-1])
            vev1 = {}
            vmm1 = 0
            for mt in range(NT):
                bank = (mt % 4) * 1024
                if mt >= 4:
                    pr.wait("tensor", sV, vev1[mt - 4])
                for nh in range(2):
                    for k in range(2):
                        fin = (k == 1)
                        if fin:
                            vmm1 = sT.bump()
                        def _mm(t, o=PSU[:, bank + 512 * nh: bank + 512 * nh + 512],
                                l=latc(k)[:, 128 * mt: 128 * mt + 128],
                                r=WAr[:, SSo + 1024 * k + 512 * nh:
                                       SSo + 1024 * k + 512 * nh + 512],
                                st=(k == 0), sp=fin, fin=fin):
                            m = t.matmul(o, l, r, start=st, stop=sp)
                            return m.then_inc(sT.h, 1) if fin else m
                        T(_mm)
                pr.wait("vector", sT, vmm1)
                vev1[mt] = V(lambda e, o=vaug_evac_out(mt), i=PSU[:, bank: bank + 1024].rearrange(
                    "p (h c) -> p h c", c=64): e.tensor_copy(o, i), chain=False)
            pr.wait("tensor", sWd, wq1_l)
            pr.wait("tensor", sV, vev1[NT - 1])

            def ev_q1(c, ps):
                pr.wait("scalar", sD, sp2a)
                vv = SC(lambda e, o=qp(c), i=ps:
                  e.activation(o, i, AF.Copy, scale=0.125), chain=False)
                return vv

            qmm1, _ = proj(qp, wsl(WA_), wq1_l, NCH, xn_rhs, ev_q1)
            slot_free["A"] = (sT, qmm1[-1])
            v1_l = wload("A", "v1", 8192)

            pr.barrier((sT, sV, sS, sG))
            kept1 = lambda h: list(range(NT))

            def act1(mc):
                j0 = (128 * mc) // SPW
                return (j0, mc - j0 * (SPW // 128))

            ab1 = lambda h, mc: None
            lpv1, yn1, lsc1 = head_loop(1, kept1, act1, ab1, pads1)

            rl1a = dsy(A16f[:, VA // 2: VA // 2 + 4 * S].rearrange("p (c t) -> p c t", t=S),
                       dscr[0:4].rearrange("c p t -> p c t"), wait=((sT, lpv1),))
            rl1b = dsy(R32[:, 4 * S: 8 * S].rearrange("p (c t) -> p c t", t=S),
                       dscr[4:8].rearrange("c p t -> p c t"))

            ep2 = out_proj(wsl(WB_), wout_l, res_mixed, None,
                           ((sD, rl1a), (sD, rl1b)), yn1)
            slot_free["B"] = (sT, sT.v)
            u2_l = wload("B", "u2", 8192)
            slot_free["SS"] = (sT, sT.v)
            u1_l = wload("SS", "u1", 2048, sub=0)
            v2_l = wload("SS", "v2", 2048, sub=2048)

            ln4 = ln_phase(res_mixed, ready=lambda c: [(sV, ep2[c])],
                           psum_free=(ep2[6], ep2[7]))
            ln4_T = sT.v
            b1_l = d16(b1row, dW["b1r"], wait=((sT, ln4_T), (sV, ln4)))
            b2_l = d16(b2row, dW["b2r"])
            for wl in (u1_l, v1_l, u2_l, v2_l, b1_l, b2_l):
                pr.wait("tensor", sWd, wl)
            pr.wait("tensor", sV, ln4)
            prev_span_done = 0
            prev_qev1 = 0
            for j in range(NSP):
                pr.wait("tensor", sS, prev_qev1)
                pev = [0, 0]
                for c in range(2):
                    pmm = 0
                    for k in range(NCH):
                        fin = (k == NCH - 1)
                        if fin:
                            pmm = sT.bump()
                        def _mm(t, o=PSU[:, 512 * c: 512 * c + 512],
                                l=WAr[:, SSo + RANK * k + 128 * c: SSo + RANK * k + 128 * c + 128],
                                r=xn_rhs(k, j), st=(k == 0), sp=fin, fin=fin):
                            m = t.matmul(o, l, r, start=st, stop=sp)
                            return m.then_inc(sT.h, 1) if fin else m
                        T(_mm)
                    pr.wait("scalar", sT, pmm)
                    pev[c] = SC(lambda e, o=Pt[c], i=PSU[:, 512 * c: 512 * c + 512]:
                      e.activation(o, i, AF.Copy), chain=False)
                pr.wait("tensor", sS, pev[1])
                hev = {}
                for g in range(8):
                    half = 2048 * (g % 2)
                    if g >= 2:
                        pr.wait("tensor", sS, hev[g - 2])
                    elif g == 0 and prev_span_done:
                        pr.wait("tensor", sV, prev_span_done)
                    hmm = 0
                    for q in range(4):
                        hc = 4 * g + q
                        for k in range(2):
                            T(lambda t, o=PSU[:, half + 512 * q: half + 512 * q + 512],
                              l=WAr[:, WA_ + DFF * k + 128 * hc: WA_ + DFF * k + 128 * hc + 128],
                              r=Pt[k], st=(k == 0): t.matmul(o, l, r, start=st, stop=False))
                        fin = (q == 3)
                        if fin:
                            hmm = sT.bump()
                        def _mm(t, o=PSU[:, half + 512 * q: half + 512 * q + 512],
                                l=b1row[:, 128 * hc: 128 * hc + 128], fin=fin):
                            m = t.matmul(o, l, cones[:, 0:512], start=False, stop=True)
                            return m.then_inc(sT.h, 1) if fin else m
                        T(_mm)
                    pr.wait("scalar", sT, hmm)
                    ho = (R16[:, 2048 * g: 2048 * (g + 1)] if H_in_qp
                          else A16[:, H_off + 2048 * g: H_off + 2048 * (g + 1)])
                    hev[g] = SC(lambda e, o=ho, i=PSU[:, half: half + 2048]:
                                e.activation(o, i, AF.Gelu_apprx_tanh), chain=False)
                pr.wait("tensor", sS, hev[6])
                pr.wait("tensor", sS, hev[7])
                qev_ = [0, 0]
                for c in range(2):
                    qmm_ = 0
                    for k in range(32):
                        fin = (k == 31)
                        if fin:
                            qmm_ = sT.bump()
                        def _mm(t, o=PSU[:, 512 * c: 512 * c + 512],
                                l=WAr[:, WB_ + RANK * k + 128 * c: WB_ + RANK * k + 128 * c + 128],
                                r=Hch(k), st=(k == 0), sp=fin, fin=fin):
                            m = t.matmul(o, l, r, start=st, stop=sp)
                            return m.then_inc(sT.h, 1) if fin else m
                        T(_mm)
                    pr.wait("scalar", sT, qmm_)
                    qev_[c] = SC(lambda e, o=Qt[c], i=PSU[:, 512 * c: 512 * c + 512]:
                      e.activation(o, i, AF.Copy), chain=False)
                pr.wait("tensor", sS, qev_[1])
                ep_oc = [0] * NCH
                for oc in range(NCH):
                    bank = 1024 + 512 * (oc % 2)
                    if oc >= 2:
                        pr.wait("tensor", sV, ep_oc[oc - 2])
                    for k in range(2):
                        T(lambda t, o=PSU[:, bank: bank + 512],
                          l=WAr[:, SSo + 2048 + 1024 * k + 128 * oc: SSo + 2048 + 1024 * k + 128 * oc + 128],
                          r=Qt[k], st=(k == 0): t.matmul(o, l, r, start=st, stop=False))
                    fo = sT.bump()
                    T(lambda t, o=PSU[:, bank: bank + 512], l=b2row[:, 128 * oc: 128 * oc + 128]:
                      t.matmul(o, l, cones[:, 0:512], start=False, stop=True).then_inc(sT.h, 1))
                    pr.wait("vector", sT, fo)
                    ep_oc[oc] = V(lambda e, o=res_mixed(oc)[:, j * SPW:(j + 1) * SPW], i=PSU[:, bank: bank + 512]:
                      e.tensor_tensor(o, o, i, ALU.add), chain=False)
                prev_span_done = ep_oc[NCH - 1]
                prev_qev1 = qev_[1]
                prev_psum_free[0] = ep_oc[NCH - 2]
                prev_psum_free[1] = ep_oc[NCH - 1]
                if j == NSP - 1:
                    prev_hch_free = qmm_
            slot_free["A"] = (sT, sT.v)
            slot_free["B"] = (sT, sT.v)
            slot_free["SS"] = (sT, sT.v)

            dyr = dy[s].rearrange("(c p) t -> p c t", p=128)
            tailT = sT.v
            if s + 1 < nseq:
                dxn = dx[s + 1].rearrange("(c p) t -> p c t", p=128)
                nv = d16(R32[:, 0:4 * S].rearrange("p (c t) -> p c t", t=S),
                         dxn[:, 0:4, :], wait=((sT, prev_hch_free),))
                xin_map = {c: (sWd, nv) for c in range(4)}
            for ci in range(4):
                dq = dsy if ci % 2 == 0 else d16
                dq(dyr[:, 4 + ci:5 + ci, :],
                   R32[:, (4 + ci) * S:(5 + ci) * S].rearrange("p (c t) -> p c t", t=S),
                   wait=((sV, ep_oc[4 + ci]),))
                if s + 1 < nseq:
                    lv = dq(R32[:, (4 + ci) * S:(5 + ci) * S], dxn[:, 4 + ci, :],
                            wait=((sT, tailT),))
                    xin_map[4 + ci] = ((sD if ci % 2 == 0 else sWd), lv)
            yA1 = dsy(dyr[:, 0:2, :],
                      A16f[:, VA // 2: VA // 2 + 2 * S].rearrange("p (c t) -> p c t", t=S),
                      wait=((sV, prev_span_done),))
            yA2 = d16(dyr[:, 2:4, :],
                      A16f[:, VA // 2 + 2 * S: VA // 2 + 4 * S].rearrange("p (c t) -> p c t", t=S),
                      wait=((sV, prev_span_done),))
            prev_ystoreA = (yA1, yA2)
            prev_allT = sT.v

        pr.wait("sync", sD, sD.v)
        pr.wait("sync", sWd, sWd.v)
        pr.wait("sync", sP, sP.v)

        ENGSEM = {"gpsimd": sG, "vector": sV, "scalar": sS}

        def replay(e, eng):
            sem = ENGSEM.get(eng)
            cnt = 0
            for fn, kind in pr.ops[eng]:
                if sem is not None and kind in ("op", "opnc"):
                    if cnt and kind == "op":
                        e.wait_ge(sem.h, cnt)
                    r = fn(e)
                    r.then_inc(sem.h, 1)
                    cnt += 1
                else:
                    fn(e)

        @block.sync
        def _(e):
            replay(e, "sync")

        @block.gpsimd
        def _(e):
            replay(e, "gpsimd")

        @block.tensor
        def _(e):
            replay(e, "tensor")

        @block.vector
        def _(e):
            replay(e, "vector")

        @block.scalar
        def _(e):
            replay(e, "scalar")

    return nc


def _chunked(a, nck):
    K, N = a.shape
    assert K == 128 * nck
    return np.ascontiguousarray(a.reshape(nck, 128, N).transpose(1, 0, 2).reshape(128, nck * N))


def prep_weights(i, S):
    NT = S // 128
    f32 = lambda a: np.ascontiguousarray(np.asarray(a, dtype=np.float32))
    out = {}
    out["wq"] = _chunked(f32(i["L0_attn_Wq"]), 8)
    out["wk"] = _chunked(f32(i["L0_attn_Wk"]), 8)
    out["wv"] = _chunked(f32(i["L0_attn_Wv"]), 8)
    out["wo"] = _chunked(f32(i["L0_attn_Wo"]), 8)
    out["wt"] = _chunked(np.ascontiguousarray(f32(i["L0_ffn_weight"]).T), 8)
    out["ut"] = _chunked(np.ascontiguousarray(f32(i["L0_ffn_lora_U"]).reshape(128, 1024).T), 8)
    out["vfl"] = np.ascontiguousarray(f32(i["L0_ffn_lora_V"]).transpose(0, 2, 1).reshape(128, 1024))
    out["gwt"] = _chunked(np.ascontiguousarray(f32(i["L0_ffn_gate_w"]).T), 8)
    out["wq1"] = _chunked(f32(i["L1_attn_Wq"]).reshape(1024, 1024), 8)
    out["down"] = _chunked(f32(i["L1_attn_Wkv_down"]), 8)
    out["kup"] = _chunked(f32(i["L1_attn_Wk_up"]).reshape(256, 1024), 2)
    out["vup"] = _chunked(f32(i["L1_attn_Wv_up"]).reshape(256, 1024), 2)
    out["wout"] = _chunked(f32(i["L1_attn_Wout"]).reshape(1024, 1024), 8)
    out["u1"] = _chunked(f32(i["L1_ffn_U1"]), 8)
    out["v1"] = _chunked(f32(i["L1_ffn_V1"]), 2)
    out["u2"] = _chunked(f32(i["L1_ffn_U2"]), 32)
    out["v2"] = _chunked(f32(i["L1_ffn_V2"]), 2)
    out["bqp"] = np.ascontiguousarray(f32(i["L0_attn_bq"]).reshape(8, 128).T)
    out["bkp"] = np.ascontiguousarray(f32(i["L0_attn_bk"]).reshape(8, 128).T)
    out["bop"] = np.ascontiguousarray(f32(i["L0_attn_bo"]).reshape(8, 128).T)
    out["bvr"] = f32(i["L0_attn_bv"]).reshape(1, 1024)
    out["b1r"] = f32(i["L1_ffn_b1"]).reshape(1, DFF)
    out["b2r"] = f32(i["L1_ffn_b2"]).reshape(1, 1024)
    sl = slopes16()
    out["cb0"] = np.broadcast_to(
        (-(C0 + sl[:, None] * (S - 1))), (16, S)).astype(bfnp)
    cal = np.zeros((128, 16 * NT), np.float32)
    p = np.arange(128, dtype=np.float64)
    for h in range(16):
        for mc in range(NT):
            cal[:, NT * h + mc] = (sl[h] * (128 * mc + p)).astype(np.float32)
    out["calibi"] = cal
    out["cid"] = np.eye(128, dtype=bfnp)
    out["cmaskL"] = (-30000.0 * np.eye(128)).astype(bfnp)
    out["cmaskR"] = (np.arange(128)[None, :] < np.arange(128)[:, None]).astype(bfnp)
    out["conesb"] = np.ones((128, 384), bfnp)
    rep = np.zeros((8, 128), np.float32)
    for e in range(8):
        rep[e, 16 * e:16 * e + 16] = 1.0 / 16.0
    out["crep"] = rep.astype(bfnp)
    out["cones"] = np.ones((1, 512), bfnp)
    cd = np.zeros((128, 128), np.float32)
    cd[:, 0] = 1.0 / 1024.0
    out["conesd"] = cd.astype(bfnp)
    sf = np.zeros((128, 128), np.float32); sf[0, :] = 1.0
    out["selF"] = sf.astype(bfnp)
    sa = np.zeros((128, 128), np.float32); sa[0, 0:64] = 1.0; sa[32, 64:128] = 1.0
    out["selA"] = sa.astype(bfnp)
    sb = np.zeros((128, 128), np.float32); sb[0, 64:128] = 1.0
    out["selB"] = sb.astype(bfnp)
    return out


def kernel(**inputs):
    S = 2048
    nseq = 2
    ncores = 8
    w = prep_weights(inputs, S)
    x = np.asarray(inputs["L0_attn_x"], dtype=np.float32)
    nc = build(S, nseq)
    in_maps = []
    for c in range(ncores):
        m = dict(w)
        m["x"] = np.ascontiguousarray(x[2 * c: 2 * c + 2].transpose(0, 2, 1))
        in_maps.append(m)
    res = run_bass_kernel_spmd(nc, in_maps, core_ids=list(range(ncores)))
    out = np.empty((16, S, D), np.float32)
    for c in range(ncores):
        yv = np.asarray(res.results[c]["y"])
        out[2 * c: 2 * c + 2] = yv.transpose(0, 2, 1)
    return out


# version 36 (final)
# speedup vs baseline: 1.0096x; 1.0096x over previous
import numpy as np
import ml_dtypes
import concourse.bass as bass
import concourse.mybir as mybir
from concourse.bass_utils import run_bass_kernel_spmd
from contextlib import ExitStack

F32 = mybir.dt.float32
BF16 = mybir.dt.bfloat16
AF = mybir.ActivationFunctionType
ALU = mybir.AluOpType
AX = mybir.AxisListType

DEBUG_TAP = False
D = 1024
H = 16
NCH = 8
EPS = 1e-5
C0 = 10.0
C1 = 15.0
DFF = 4096
LD = 256
RANK = 256

bfnp = ml_dtypes.bfloat16


def slopes16():
    x = (2.0 ** 8) ** (1.0 / 16)
    return np.array([1.0 / x ** (i + 1) for i in range(16)], np.float64)


def l0_mstart(S):
    NT = S // 128
    out = []
    for sl in slopes16():
        w = int(np.ceil((36.0 / sl) / 128.0))
        out.append(NT - max(1, min(NT, w)))
    return out


class Ctr:
    __slots__ = ("h", "v", "name")

    def __init__(self, name):
        self.h = None
        self.v = 0
        self.name = name

    def bump(self, n=1):
        self.v += n
        return self.v


class Prog:
    def __init__(self):
        self.ops = {"sync": [], "gpsimd": [], "tensor": [], "vector": [], "scalar": []}

    def emit(self, eng, fn, kind="op"):
        self.ops[eng].append((fn, kind))

    def wait(self, eng, ctr, val):
        if val > 0:
            self.ops[eng].append(
                (lambda e, c=ctr, v=val: e.wait_ge(c.h, v), "wait"))

    def barrier(self, ctrs, engines=("tensor", "vector", "scalar")):
        vals = [(c, c.v) for c in ctrs]
        for e in engines:
            for c, v in vals:
                self.wait(e, c, v)


def build(S, nseq):
    NT = S // 128
    NSP = max(1, S // 512)
    SPW = min(512, S)
    MSTART = l0_mstart(S)

    nc = bass.Bass("TRN2", target_bir_lowering=False)

    dx = nc.dram_tensor("x", [nseq, D, S], F32, kind="ExternalInput").ap()
    dy = nc.dram_tensor("y", [nseq, D, S], F32, kind="ExternalOutput").ap()
    dscr = nc.dram_tensor("scr", [NCH, 128, S], F32).ap()
    ddbg = nc.dram_tensor("dbg", [3, 128, S], F32, kind="ExternalOutput").ap() if DEBUG_TAP else None

    def din(name, shape, dt=F32):
        return nc.dram_tensor(name, list(shape), dt, kind="ExternalInput").ap()

    dW = {
        "wq": din("wq", (128, NCH * 1024)), "wk": din("wk", (128, NCH * 1024)),
        "wv": din("wv", (128, NCH * 1024)), "wo": din("wo", (128, NCH * 1024)),
        "wt": din("wt", (128, NCH * 1024)),
        "ut": din("ut", (128, NCH * 128)), "vfl": din("vfl", (128, 1024)),
        "gwt": din("gwt", (128, NCH * 8)),
        "wq1": din("wq1", (128, NCH * 1024)),
        "down": din("down", (128, NCH * LD)),
        "kup": din("kup", (128, 2 * 1024)), "vup": din("vup", (128, 2 * 1024)),
        "wout": din("wout", (128, NCH * 1024)),
        "u1": din("u1", (128, NCH * RANK)), "v1": din("v1", (128, 2 * DFF)),
        "u2": din("u2", (128, 32 * RANK)), "v2": din("v2", (128, 2 * 1024)),
        "bqp": din("bqp", (128, 8)), "bkp": din("bkp", (128, 8)), "bop": din("bop", (128, 8)),
        "bvr": din("bvr", (1, 1024)), "b1r": din("b1r", (1, DFF)), "b2r": din("b2r", (1, 1024)),
        "cb0": din("cb0", (16, S), BF16),
        "calibi": din("calibi", (128, 16 * NT)),
        "cid": din("cid", (128, 128), BF16),
        "cmaskL": din("cmaskL", (128, 128), BF16), "cmaskR": din("cmaskR", (128, 128), BF16),
        "conesb": din("conesb", (128, 384), BF16),
        "crep": din("crep", (8, 128), BF16), "cones": din("cones", (1, 512), BF16),
        "conesd": din("conesd", (128, 128), BF16),
        "selF": din("selF", (128, 128), BF16), "selA": din("selA", (128, 128), BF16),
        "selB": din("selB", (128, 128), BF16),
    }

    pr = Prog()
    sT = Ctr("sT")
    sV = Ctr("sV")
    sS = Ctr("sS")
    sG = Ctr("sG")
    sWd = Ctr("sWd")
    sD = Ctr("sD")
    sP = Ctr("sP")
    CTRS = (sT, sV, sS, sG, sWd, sD, sP)

    with ExitStack() as stack:
        en = stack.enter_context
        R32 = en(nc.sbuf_tensor("R32", [128, NCH * S], F32))
        A16cols = 8 * S + NT * 1040 + 64 + max(4 * S, 3072 + DFF)
        H_in_qp = (8 * S >= 32 * 512)
        if not H_in_qp:
            H_off = A16cols
            A16cols += 32 * 512
        A16 = en(nc.sbuf_tensor("A16", [128, A16cols], BF16))
        WAr = en(nc.sbuf_tensor("WAr", [128, 20480], BF16))
        SM32 = en(nc.sbuf_tensor("SM32", [128, S], F32))
        SAL = en(nc.sbuf_tensor("SAL", [128, 16 * NT], F32))
        SBI = en(nc.sbuf_tensor("SBI", [128, 32], F32))
        CN = en(nc.sbuf_tensor("CN", [128, 2048], BF16))
        RW16 = en(nc.sbuf_tensor("RW16", [128, 2 * S], BF16))
        PSU = en(nc.psum_tensor("PSU", [128, 4096], F32))
        for c in CTRS:
            c.h = en(nc.semaphore(c.name))
        block = en(nc.Block())

        XN = 0
        VA = 8 * S
        AUG = 8 * S + NT * 1040 + 64

        R16 = R32.ap().bitcast(BF16)
        A16f = A16.ap().bitcast(F32)

        res_home = lambda c: R32[:, S * c: S * (c + 1)]
        res_vaug = lambda c: A16f[:, (VA // 2) + S * c: (VA // 2) + S * (c + 1)]
        qp = lambda c: R16[:, S * c: S * (c + 1)]
        kp = lambda c: R16[:, 8 * S + S * c: 8 * S + S * (c + 1)]
        ybuf = lambda c: R16[:, S * c: S * (c + 1)]
        xnc = lambda c: A16[:, XN + S * c: XN + S * (c + 1)]

        ptile = lambda i: A16[:, XN + S * i: XN + S * (i + 1)]
        nbc16 = lambda: A16[:, XN + 3 * S: XN + 4 * S]

        inv16 = RW16[0:1, 0:S]
        c16 = RW16[0:1, S: 2 * S]
        dinvr = (RW16[0:1, 0:S], RW16[32:33, 0:S])

        def vaug_lhs(mt, h):
            off = VA + 1040 * mt + 65 * h
            return A16[:, off: off + 128]

        def vaug_evac_out(mt):
            return A16[:, VA + 1040 * mt: VA + 1040 * mt + 1040].rearrange(
                "p (h c) -> p h c", c=65)[:, :, 0:64]

        vaug_ones_all = A16[:, VA: VA + 1040 * NT].rearrange(
            "p (h c) -> p h c", c=65)[:, :, 64:65]

        qaug = [A16[:, AUG + S * i: AUG + S * (i + 1)] for i in range(2)]
        kaug = [A16[:, AUG + 2 * S + S * i: AUG + 3 * S + S * i] for i in range(2)]
        aug_pad = [A16[64:128, AUG + S * i: AUG + S * (i + 1)] for i in range(4)]
        qaug_brow = [A16[64:65, AUG + S * i: AUG + S * (i + 1)] for i in range(2)]
        kaug_ones = A16[64:65, AUG + 2 * S: AUG + 4 * S]

        lnsl = lambda i: A16[:, AUG + S * i: AUG + S * (i + 1)]

        Pt = [A16[:, AUG + 512 * i: AUG + 512 * (i + 1)] for i in range(2)]
        Qt = [A16[:, AUG + 1024 + 512 * i: AUG + 1024 + 512 * (i + 1)] for i in range(2)]
        b1row = A16[0:1, AUG + 2048: AUG + 2048 + DFF]
        b2row = A16[0:1, AUG + 2048 + DFF: AUG + 3072 + DFF]
        bvrow = A16[0:1, AUG: AUG + 1024]

        g16 = lambda mt: R16[:, 3 * S + 8 * mt: 3 * S + 8 * mt + 8]
        gTrow = R16[0:8, 2 * S: 3 * S]
        T2b = R16[:, 0:S]
        grep16 = R16[:, S: 2 * S]
        latc = lambda c: R16[:, (5 + c) * S: (6 + c) * S]
        if H_in_qp:
            Hch = lambda hc: R16[:, 512 * hc: 512 * (hc + 1)]
        else:
            Hch = lambda hc: A16[:, H_off + 512 * hc: H_off + 512 * (hc + 1)]

        WA_ = 0
        WB_ = 8192
        SSo = 16384

        m2row = SM32[0:1, :]
        dsc = SM32[96:97, :]
        vepsrow = SM32[32:33, :]
        lnrow = SM32[64:65, :]
        gexp = lambda mt: SM32[:, 8 * mt: 8 * mt + 8]
        gsum = lambda mt: SM32[:, 128 + mt: 129 + mt]
        grin = lambda mt: SM32[:, 160 + mt: 161 + mt]

        ident = CN[:, 0:128]
        cmaskL = CN[:, 128:256]
        crep = CN[0:8, 256:384]
        cones = CN[0:1, 384:896]
        conesd = CN[:, 896:1024]
        selF = CN[:, 1024:1152]
        selA = CN[:, 1152:1280]
        selB = CN[:, 1280:1408]
        cmaskR = CN[:, 1408:1536]
        conesb = CN[:, 1536:1920]

        T = lambda fn: pr.emit("tensor", fn)

        def V(fn, chain=True):
            pr.emit("vector", fn, "op" if chain else "opnc")
            return sV.bump()

        def SC(fn, chain=True):
            pr.emit("scalar", fn, "op" if chain else "opnc")
            return sS.bump()

        def G(fn):
            pr.emit("gpsimd", fn)
            return sG.bump()

        SY = lambda fn: pr.emit("sync", fn)

        def d16(dst, src, wait=()):
            pr.wait("gpsimd", sWd, sWd.v)
            for (c, vv) in wait:
                pr.wait("gpsimd", c, vv)
            v = sWd.bump(16)
            pr.emit("gpsimd",
                    lambda g, d=dst, s=src: g.dma_start(out=d, in_=s).then_inc(sWd.h, 16),
                    kind="dma")
            return v

        def dsy(dst, src, wait=()):
            pr.wait("sync", sD, sD.v)
            for (c, vv) in wait:
                pr.wait("sync", c, vv)
            v = sD.bump(16)
            SY(lambda e, d=dst, s=src: e.dma_start(out=d, in_=s).then_inc(sD.h, 16))
            return v

        def dscq(dst, src, wait=()):
            pr.wait("scalar", sP, sP.v)
            for (c, vv) in wait:
                pr.wait("scalar", c, vv)
            v = sP.bump(16)
            pr.emit("scalar",
                    lambda e, d=dst, s=src: e.dma_start(out=d, in_=s).then_inc(sP.h, 16),
                    kind="dma")
            return v

        d16(ident, dW["cid"])
        d16(cmaskL, dW["cmaskL"])
        d16(cmaskR, dW["cmaskR"])
        d16(conesb, dW["conesb"])
        d16(crep, dW["crep"])
        d16(cones, dW["cones"])
        d16(conesd, dW["conesd"])
        d16(selF, dW["selF"])
        d16(selA, dW["selA"])
        consts_w = d16(selB, dW["selB"])
        G(lambda g: g.memset(RW16[:, :], 0.0))
        G(lambda g: g.memset(A16[:, AUG - 64: AUG], 0.0))
        dsy(SAL[:, :], dW["calibi"])
        dsy(SBI[:, 0:8], dW["bqp"])
        dsy(SBI[:, 8:16], dW["bkp"])
        bias_d = dsy(SBI[:, 16:24], dW["bop"])
        pr.wait("scalar", sD, bias_d)
        SC(lambda e: e.activation(SBI[:, 24:32], SBI[:, 0:8], AF.Copy, scale=0.125))

        slot_free = {"A": (sT, 0), "B": (sT, 0), "SS": (sT, 0)}

        def wload(slot, dname, cols, sub=0):
            base = {"A": WA_, "B": WB_, "SS": SSo}[slot]
            ctr, val = slot_free[slot]
            dst = WAr[:, base + sub: base + sub + cols]
            return d16(dst, dW[dname], wait=((ctr, val),))

        def wsl(base):
            return lambda k, c: WAr[:, base + 1024 * k + 128 * c: base + 1024 * k + 128 * c + 128]

        def ln_phase(res_ap, ready=None, psum_free=None, xn_safe=True):
            if ready is None:
                pr.barrier(CTRS)
            mm_done = [0] * NCH
            for c in range(NCH):
                x16 = xnc(c) if xn_safe else lnsl(c % 2)
                xsq = lnsl(2 + c % 2)
                if ready is not None:
                    for ctr, val in ready(c):
                        pr.wait("vector", ctr, val)
                        pr.wait("scalar", ctr, val)
                if c >= 2:
                    pr.wait("vector", sT, mm_done[c - 2])
                    pr.wait("scalar", sT, mm_done[c - 2])
                gvv = SC(lambda e, o=x16, i=res_ap(c): e.activation(o, i, AF.Copy),
                         chain=False)
                svv = SC(lambda e, o=xsq, i=res_ap(c): e.activation(o, i, AF.Square),
                         chain=False)
                pr.wait("tensor", sS, gvv)
                pr.wait("tensor", sS, svv)
                if c == 0:
                    pr.wait("tensor", sWd, consts_w)
                    if psum_free is not None:
                        pr.wait("tensor", sV, psum_free[0])
                        pr.wait("tensor", sV, psum_free[1])
                for j in range(NSP):
                    T(lambda t, o=PSU[:, j * SPW:(j + 1) * SPW], r=x16[:, j * SPW:(j + 1) * SPW],
                      st=(c == 0), sp=(c == NCH - 1): t.matmul(o, conesd, r, start=st, stop=sp))
                for j in range(NSP):
                    fin = (j == NSP - 1)
                    if fin:
                        mm_done[c] = sT.bump()
                    def _mm(t, o=PSU[:, 2048 + j * SPW: 2048 + (j + 1) * SPW],
                            r=xsq[:, j * SPW:(j + 1) * SPW], st=(c == 0),
                            sp=(c == NCH - 1), fin=fin):
                        m = t.matmul(o, conesd, r, start=st, stop=sp)
                        return m.then_inc(sT.h, 1) if fin else m
                    T(_mm)
            pr.wait("scalar", sT, mm_done[NCH - 1])
            s_m2 = SC(lambda e: e.activation(m2row, PSU[0:1, 0:S], AF.Square))
            pr.wait("vector", sS, s_m2)
            v_veps = V(lambda e: e.scalar_tensor_tensor(vepsrow, PSU[0:1, 2048:2048 + S], EPS, m2row,
                                               ALU.add, ALU.subtract))
            pr.wait("scalar", sV, v_veps)
            s_ln = SC(lambda e: e.activation(lnrow, vepsrow, AF.Ln))
            pr.wait("scalar", sS, s_ln)
            s_inv = SC(lambda e: e.activation(inv16, lnrow, AF.Exp, scale=-0.5))
            pr.wait("vector", sS, s_inv)
            v_c16 = V(lambda e: e.scalar_tensor_tensor(c16, PSU[0:1, 0:S], -1.0, inv16,
                                               ALU.mult, ALU.mult))
            pr.wait("tensor", sS, s_inv)
            ab_done = 0
            for j in range(NSP):
                fin = (j == NSP - 1)
                if fin:
                    ab_done = sT.bump()
                def _mm(t, o=PSU[:, 2048 + j * SPW: 2048 + (j + 1) * SPW],
                        r=RW16[:, j * SPW:(j + 1) * SPW], fin=fin):
                    m = t.matmul(o, selF, r, start=True, stop=True)
                    return m.then_inc(sT.h, 1) if fin else m
                T(_mm)
            pr.wait("tensor", sV, v_c16)
            cb_done = 0
            for j in range(NSP):
                fin = (j == NSP - 1)
                if fin:
                    cb_done = sT.bump()
                def _mm(t, o=PSU[:, j * SPW:(j + 1) * SPW],
                        r=RW16[:, S + j * SPW: S + (j + 1) * SPW], fin=fin):
                    m = t.matmul(o, selF, r, start=True, stop=True)
                    return m.then_inc(sT.h, 1) if fin else m
                T(_mm)
            ab16 = lnsl(0)
            cb16 = lnsl(1)
            pr.wait("vector", sT, ab_done)
            V(lambda e: e.tensor_copy(ab16, PSU[:, 2048:2048 + S]), chain=False)
            for c in range(NCH):
                if xn_safe:
                    V(lambda e, o=xnc(c): e.tensor_tensor(o, o, ab16, ALU.mult),
                      chain=(c == 0))
                else:
                    V(lambda e, o=xnc(c), i=res_ap(c): e.tensor_tensor(o, i, ab16, ALU.mult),
                      chain=(c == 0))
                if c == 0:
                    pr.wait("vector", sT, cb_done)
                    V(lambda e: e.tensor_copy(cb16, PSU[:, 0:S]), chain=False)
            out = 0
            for c in range(NCH):
                out = V(lambda e, o=xnc(c): e.tensor_tensor(o, o, cb16, ALU.add),
                        chain=False)
            return out

        def proj(dst_fn, w_ap, w_loaded, nk, rhs_fn, evac, bias_aug=None,
                 prev_half_free=(0, 0), ln_wait=0, ev_ctr=None):
            if ev_ctr is None:
                ev_ctr = sS
            lastmm = [0] * NCH
            ev = [0] * NCH
            pr.wait("tensor", sWd, w_loaded)
            if ln_wait:
                pr.wait("tensor", sV, ln_wait)
            for c in range(NCH):
                half = 2048 * (c % 2)
                if c >= 2:
                    pr.wait("tensor", ev_ctr, ev[c - 2])
                elif prev_half_free[c % 2]:
                    pr.wait("tensor", ev_ctr, prev_half_free[c % 2])
                for j in range(NSP):
                    last_pass = (bias_aug is None)
                    for k in range(nk):
                        fin = (j == NSP - 1 and k == nk - 1 and last_pass)
                        if fin:
                            lastmm[c] = sT.bump()
                        def _mm(t, o=PSU[:, half + j * SPW: half + (j + 1) * SPW],
                                l=w_ap(k, c), r=rhs_fn(k, j), st=(k == 0),
                                sp=(k == nk - 1 and last_pass), fin=fin):
                            m = t.matmul(o, l, r, start=st, stop=sp)
                            return m.then_inc(sT.h, 1) if fin else m
                        T(_mm)
                    if bias_aug is not None:
                        lrow, rrow = bias_aug
                        fin = (j == NSP - 1)
                        if fin:
                            lastmm[c] = sT.bump()
                        def _mm(t, o=PSU[:, half + j * SPW: half + (j + 1) * SPW],
                                l=lrow(c), r=rrow(j), fin=fin):
                            m = t.matmul(o, l, r, start=False, stop=True)
                            return m.then_inc(sT.h, 1) if fin else m
                        T(_mm)
                pr.wait("scalar" if ev_ctr is sS else "vector", sT, lastmm[c])
                ev[c] = evac(c, PSU[:, 2048 * (c % 2): 2048 * (c % 2) + S])
            return lastmm, ev

        def emit_pads(layer):
            pr.wait("vector", sD, sD.v)
            pr.wait("vector", sS, sS.v)
            for i in range(4):
                V(lambda e, o=aug_pad[i]: e.memset(o, 0.0), chain=False)
            if layer == 1:
                V(lambda e: e.memset(qaug_brow[0], -C1), chain=False)
                V(lambda e: e.memset(qaug_brow[1], -C1), chain=False)
            return V(lambda e: e.memset(kaug_ones, 1.0), chain=False)

        def head_loop(layer, kept_fn, act_fn, bias_fn, pads_vv):
            state = {"y_free": 0, "ynorm_last": 0}
            pend_norm = [None]
            pend_copies = [None]
            dinv_done = [(sV, 0), (sV, 0)]
            ptile_last = [0, 0, 0]
            last_score_prev_pair = 0
            prev_exp_half = [0, 0]
            last_pv = 0
            ynorm_last = 0

            def emit_pv(h_, gseg, exp_v, zv_, pbuf_, first_group):
                if first_group and pend_norm[0] is not None:
                    state["ynorm_last"] = pend_norm[0]()
                    pend_norm[0] = None
                pr.wait("tensor", sS, exp_v)
                if zv_:
                    pr.wait("tensor", sV, zv_)
                if first_group and state["y_free"]:
                    pr.wait("tensor", sV, state["y_free"])
                pvv = 0
                kept0_ = kept_fn(h_)[0]
                keptN_ = kept_fn(h_)[-1]
                for si, (mc, j, pc) in enumerate(gseg):
                    fin = (si == len(gseg) - 1)
                    if fin:
                        pvv = sT.bump()
                    if layer == 1:
                        st = (mc == 0)
                        sp = (mc == min(4 * j + 3, NT - 1))
                    else:
                        st = (mc == kept0_)
                        sp = (mc == keptN_)
                    def _mm(t, o=PSU[:, 2048 + j * SPW: 2048 + (j + 1) * SPW],
                            l=vaug_lhs(mc, h_), rr=pbuf_[:, pc:pc + SPW],
                            st=st, sp=sp, fin=fin):
                        m = t.matmul(o, l, rr, start=st, stop=sp)
                        return m.then_inc(sT.h, 1) if fin else m
                    T(_mm)
                return pvv

            def emit_copies(p_, score_wait):
                pr.wait("vector", sT, score_wait)
                qsrc = qp(p_)
                ksrc = kp(p_)
                V(lambda e, o=qaug[0][0:64, :], i=qsrc[0:64, :]: e.tensor_copy(o, i), chain=False)
                V(lambda e, o=qaug[1][0:64, :], i=qsrc[64:128, :]: e.tensor_copy(o, i), chain=False)
                V(lambda e, o=kaug[0][0:64, :], i=ksrc[0:64, :]: e.tensor_copy(o, i), chain=False)
                if layer == 1:
                    cv_ = V(lambda e, o=kaug[1][0:64, :], i=ksrc[64:128, :]: e.tensor_copy(o, i),
                            chain=False)
                    bload_ = 0
                else:
                    cv_ = V(lambda e, o=kaug[1][0:64, :], i=ksrc[64:128, :]: e.tensor_copy(o, i),
                            chain=False)
                    dsy(qaug_brow[0], dW["cb0"][2 * p_:2 * p_ + 1, :],
                        wait=((sT, score_wait), (sV, pads_v)))
                    bload_ = dsy(qaug_brow[1], dW["cb0"][2 * p_ + 1:2 * p_ + 2, :])
                return cv_, bload_

            pads_v = pads_vv
            for p in range(NCH):
                if pend_copies[0] is not None:
                    cv, bload = pend_copies[0]
                    pend_copies[0] = None
                else:
                    cv, bload = emit_copies(p, last_score_prev_pair)
                yev = 0
                for hh in range(2):
                    h = 2 * p + hh
                    kept = kept_fn(h)
                    pend_pv = []
                    sc_v = 0
                    groups = []
                    if layer == 0:
                        for mc in kept:
                            groups.append([(mc, j, j * SPW) for j in range(NSP)])
                    else:
                        mc = 0
                        while mc < NT:
                            j0 = (128 * mc) // SPW
                            width = (NSP - j0) * SPW
                            nbatch = max(1, (NSP * SPW) // width) if width <= 1024 else 1
                            gseg = []
                            base = 0
                            for bmc in range(mc, min(NT, mc + nbatch)):
                                for j in range(j0, NSP):
                                    gseg.append((bmc, j, base))
                                    base += SPW
                            groups.append(gseg)
                            mc = min(NT, mc + nbatch)
                    for idx, gseg in enumerate(groups):
                        pbuf = ptile(idx % 3)
                        if idx == 0:
                            pr.wait("tensor", sV, cv)
                            if bload:
                                pr.wait("tensor", sD, bload)
                        lo = min(pc for _, _, pc in gseg)
                        hi = max(pc for _, _, pc in gseg) + SPW
                        mid = min(2 * SPW, S)
                        halves = []
                        if lo < mid:
                            halves.append((0, lo, min(mid, hi)))
                        if hi > mid:
                            halves.append((1, max(lo, mid), hi))
                        ptl = ptile_last[idx % 3]
                        ev_ = 0
                        gzv = 0
                        for hx, (hid, a0, b0) in enumerate(halves):
                            pr.wait("tensor", sS, prev_exp_half[hid])
                            segs = [sg for sg in gseg if a0 <= sg[2] < b0]
                            exp_lo = a0
                            for si, (mc, j, pc) in enumerate(segs):
                                j0c = (128 * mc) // SPW if layer == 1 else 0
                                r = mc - 4 * j0c if layer == 1 else 0
                                diag = (layer == 1 and j == j0c)
                                last_seg = (si == len(segs) - 1)
                                fin = last_seg and not diag
                                if fin:
                                    sc_v = sT.bump()
                                def _mm(t, o=PSU[:, pc:pc + SPW],
                                        l=kaug[hh][:, 128 * mc:128 * mc + 128],
                                        rr=qaug[hh][:, j * SPW:(j + 1) * SPW],
                                        fin=fin, sp=not diag):
                                    m = t.matmul(o, l, rr, start=True, stop=sp)
                                    return m.then_inc(sT.h, 1) if fin else m
                                T(_mm)
                                if diag:
                                    if r > 0:
                                        T(lambda t, o=PSU[:, pc: pc + 128 * r],
                                          rr=conesb[:, 0:128 * r]:
                                          t.matmul(o, cmaskL, rr, start=False, stop=False))
                                    fin2 = last_seg
                                    if fin2:
                                        sc_v = sT.bump()
                                    def _mm2(t, o=PSU[:, pc + 128 * r: pc + 128 * r + 128], fin=fin2):
                                        m = t.matmul(o, cmaskL, cmaskR, start=False, stop=True)
                                        return m.then_inc(sT.h, 1) if fin else m
                                    T(_mm2)
                            pr.wait("scalar", sT, sc_v)
                            if ptl:
                                pr.wait("scalar", sT, ptl)
                            bias = bias_fn(h, gseg[0][0])
                            if bias is None:
                                ev_ = SC(lambda e, o=pbuf[:, exp_lo:b0], i=PSU[:, exp_lo:b0]:
                                         e.activation(o, i, AF.Exp), chain=False)
                            else:
                                ev_ = SC(lambda e, o=pbuf[:, exp_lo:b0], i=PSU[:, exp_lo:b0], b=bias:
                                         e.activation(o, i, AF.Exp, bias=b), chain=False)
                            prev_exp_half[hid] = ev_
                        pend_pv.append((idx, gseg, ev_, gzv, pbuf, idx == 0))
                        if len(pend_pv) > 2:
                            i_, *args = pend_pv.pop(0)
                            last_pv = emit_pv(h, *args)
                            ptile_last[i_ % 3] = last_pv
                    while pend_pv:
                        i_, *args = pend_pv.pop(0)
                        last_pv = emit_pv(h, *args)
                        ptile_last[i_ % 3] = last_pv
                    last_score_prev_pair = sc_v
                    if hh == 1 and p + 1 < NCH:
                        pend_copies[0] = emit_copies(p + 1, sc_v)
                    if hh == 0:
                        pr.wait("vector", sT, last_pv)
                        V(lambda e: e.tensor_copy(dsc, PSU[64:65, 2048:2048 + S]),
                          chain=False)
                    else:
                        pr.wait("scalar", sT, last_pv)
                        SC(lambda e: e.activation(m2row, PSU[64:65, 2048:2048 + S], AF.Ln))
                        def _dinv(e, o=dinvr[hh]):
                            with nc.allow_low_precision(reason="bf16 dinv is matmul-operand rounding"):
                                return e.activation(o, m2row, AF.Exp, scale=-1.0)
                        dnv = SC(_dinv)
                        dinv_done[hh] = (sS, dnv)
                        pr.wait("vector", sT, last_pv)
                    yout = ybuf(p)[0:64, :] if hh == 0 else ybuf(p)[64:128, :]
                    yev = V(lambda e, o=yout: e.tensor_copy(o, PSU[0:64, 2048:2048 + S]),
                            chain=False)
                    state["y_free"] = yev
                    if hh == 0:
                        def _dinv(e, o=dinvr[hh]):
                            with nc.allow_low_precision(reason="bf16 dinv is matmul-operand rounding"):
                                return e.reciprocal(o, dsc)
                        dnv = V(_dinv)
                        dinv_done[hh] = (sV, dnv)
                def _norm(p_=p, yev_=yev, dd=dinv_done[1], da=dinv_done[0]):
                    pr.wait("tensor", sV, yev_)
                    pr.wait("tensor", da[0], da[1])
                    pr.wait("tensor", dd[0], dd[1])
                    bcv = 0
                    for j in range(NSP):
                        fin = (j == NSP - 1)
                        if fin:
                            bcv = sT.bump()
                        def _mm(t, o=PSU[:, 2048 + j * SPW: 2048 + (j + 1) * SPW],
                                r=RW16[:, j * SPW:(j + 1) * SPW], fin=fin):
                            m = t.matmul(o, selA, r, start=True, stop=True)
                            return m.then_inc(sT.h, 1) if fin else m
                        T(_mm)
                    pr.wait("vector", sT, bcv)
                    cpv = V(lambda e: e.tensor_copy(nbc16(), PSU[:, 2048:2048 + S]),
                            chain=False)
                    nv = V(lambda e, o=ybuf(p_): e.tensor_tensor(o, o, nbc16(), ALU.mult))
                    state["y_free"] = cpv
                    return nv
                pend_norm[0] = _norm
            state["ynorm_last"] = pend_norm[0]()
            pend_norm[0] = None
            return last_pv, state["ynorm_last"], last_score_prev_pair

        def out_proj(w_ap, w_loaded, res_ap, bias_col, reload_marks, ynorm_last):
            pr.wait("tensor", sWd, w_loaded)
            pr.wait("tensor", sV, ynorm_last)
            ep = [0] * NCH
            for oc in range(NCH):
                half = 2048 * (oc % 2)
                if oc >= 2:
                    pr.wait("tensor", sV, ep[oc - 2])
                mmv = 0
                for j in range(NSP):
                    for k in range(NCH):
                        fin = (j == NSP - 1 and k == NCH - 1)
                        if fin:
                            mmv = sT.bump()
                        def _mm(t, o=PSU[:, half + j * SPW: half + (j + 1) * SPW],
                                l=w_ap(k, oc), r=ybuf(k)[:, j * SPW:(j + 1) * SPW],
                                st=(k == 0), sp=(k == NCH - 1), fin=fin):
                            m = t.matmul(o, l, r, start=st, stop=sp)
                            return m.then_inc(sT.h, 1) if fin else m
                        T(_mm)
                pr.wait("vector", sT, mmv)
                for (c_, v_) in reload_marks:
                    pr.wait("vector", c_, v_)
                if bias_col is not None:
                    ep[oc] = V(lambda e, o=res_ap(oc), i=PSU[:, half: half + S], b=bias_col(oc):
                               e.scalar_tensor_tensor(o, i, b, o, ALU.add, ALU.add))
                else:
                    ep[oc] = V(lambda e, o=res_ap(oc), i=PSU[:, half: half + S]:
                               e.tensor_tensor(o, o, i, ALU.add))
            return ep

        prev_ystoreA = 0
        prev_allT = 0
        prev_hch_free = 0
        prev_psum_free = [0, 0]
        dx0r = dx[0].rearrange("(c p) t -> p c t", p=128)
        xin_map = {}
        for ci in range(4):
            dq = dsy if ci < 2 else d16
            v = dq(R32[:, ci * S:(ci + 1) * S], dx0r[:, ci, :])
            xin_map[ci] = ((sD if ci < 2 else sWd), v)
        for ci in range(4):
            dq = dsy if ci % 2 == 0 else d16
            v = dq(R32[:, (4 + ci) * S:(5 + ci) * S], dx0r[:, 4 + ci, :])
            xin_map[4 + ci] = ((sD if ci % 2 == 0 else sWd), v)
        for s in range(nseq):
            dxr = dx[s].rearrange("(c p) t -> p c t", p=128)

            ln1 = ln_phase(res_home,
                           ready=lambda c, m=dict(xin_map), pa=prev_allT:
                               [m[c], (sT, pa)],
                           psum_free=tuple(prev_psum_free))

            wq_l = wload("A", "wq", 8192)
            wk_l = wload("B", "wk", 8192)
            xn_rhs = lambda k, j: xnc(k)[:, j * SPW:(j + 1) * SPW]

            def ev_q(c, ps):
                vv = SC(lambda e, o=qp(c), i=ps, b=SBI[:, 24 + c:25 + c]:
                  e.activation(o, i, AF.Identity, bias=b, scale=0.125), chain=False)
                return vv

            ln1_T = sT.v
            pads0 = emit_pads(0)
            qmm, qev = proj(qp, wsl(WA_), wq_l, NCH, xn_rhs, ev_q, ln_wait=ln1)
            slot_free["A"] = (sT, qmm[-1])
            wv_l = wload("A", "wv", 8192)
            bvr_l = d16(bvrow, dW["bvr"], wait=((sT, ln1_T),))
            if prev_ystoreA:
                pr.emit("gpsimd", lambda g, v=prev_ystoreA[0]: g.wait_ge(sD.h, v), kind="wait")
            gva = G(lambda g: g.memset(vaug_ones_all, 1.0))

            def ev_k(c, ps):
                vv = SC(lambda e, o=kp(c), i=ps, b=SBI[:, 8 + c:9 + c]:
                  e.activation(o, i, AF.Identity, bias=b), chain=False)
                return vv

            kmm, kev = proj(kp, wsl(WB_), wk_l, NCH, xn_rhs, ev_k,
                            prev_half_free=(qev[6], qev[7]))
            slot_free["B"] = (sT, kmm[-1])
            wo_l = wload("B", "wo", 8192)

            pr.wait("tensor", sWd, wv_l)
            pr.wait("tensor", sWd, bvr_l)
            vev = {}
            vmm_last = 0
            for mt in range(NT):
                bank = (mt % 4) * 1024
                if mt >= 4:
                    pr.wait("tensor", sV, vev[mt - 4])
                elif mt == 0:
                    pr.wait("tensor", sS, kev[6])
                elif mt == 2:
                    pr.wait("tensor", sS, kev[7])
                for nh in range(2):
                    for k in range(NCH):
                        T(lambda t, o=PSU[:, bank + 512 * nh: bank + 512 * nh + 512],
                          l=xnc(k)[:, 128 * mt: 128 * mt + 128],
                          r=WAr[:, WA_ + 1024 * k + 512 * nh: WA_ + 1024 * k + 512 * nh + 512],
                          st=(k == 0): t.matmul(o, l, r, start=st, stop=False))
                    vmm_last = sT.bump()
                    T(lambda t, o=PSU[:, bank + 512 * nh: bank + 512 * nh + 512],
                      r=bvrow[:, 512 * nh: 512 * nh + 512]:
                      t.matmul(o, cones[0:1, 0:128], r, start=False, stop=True).then_inc(sT.h, 1))
                pr.wait("vector", sT, vmm_last)
                vv = V(lambda e, o=vaug_evac_out(mt), i=PSU[:, bank: bank + 1024].rearrange(
                    "p (h c) -> p h c", c=64): e.tensor_copy(o, i), chain=False)
                vev[mt] = vv
            slot_free["A"] = (sT, vmm_last)
            wt_l = wload("A", "wt", 8192)

            pr.barrier((sT, sV, sS, sG))
            kept0 = lambda h: list(range(MSTART[h], NT))
            act0 = lambda mc: (0, 0)
            ab0 = lambda h, mc: SAL[:, NT * h + mc: NT * h + mc + 1]
            lpv0, yn0, lsc0 = head_loop(0, kept0, act0, ab0, pads0)

            rl_a = dsy(A16f[:, VA // 2: VA // 2 + 4 * S].rearrange("p (c t) -> p c t", t=S),
                       dxr[:, 0:4, :], wait=((sT, lpv0),))
            rl_b = dsy(R32[:, 4 * S: 8 * S].rearrange("p (c t) -> p c t", t=S),
                       dxr[:, 4:8, :])

            res_mixed = lambda c: res_vaug(c) if c < 4 else res_home(c)

            ep0 = out_proj(wsl(WB_), wo_l, res_mixed,
                           lambda oc: SBI[:, 16 + oc: 17 + oc],
                           ((sD, rl_a), (sD, rl_b)), yn0)
            slot_free["B"] = (sT, sT.v)

            ut_l = wload("SS", "ut", 1024, sub=0)
            vfl_l = wload("SS", "vfl", 1024, sub=1024)
            gwt_l = wload("SS", "gwt", 64, sub=2048)
            ln2 = ln_phase(res_mixed, ready=lambda c: [(sV, ep0[c])],
                           psum_free=(ep0[6], ep0[7]))

            pr.wait("tensor", sWd, gwt_l)
            pr.wait("tensor", sV, ln2)
            gate_mm = {}
            for mt in range(NT):
                for k in range(NCH):
                    fin = (k == NCH - 1)
                    if fin:
                        gate_mm[mt] = sT.bump()
                    def _mm(t, o=PSU[:, 8 * mt: 8 * mt + 8],
                            l=xnc(k)[:, 128 * mt: 128 * mt + 128],
                            r=WAr[:, SSo + 2048 + 8 * k: SSo + 2048 + 8 * k + 8],
                            st=(k == 0), sp=fin, fin=fin):
                        m = t.matmul(o, l, r, start=st, stop=sp)
                        return m.then_inc(sT.h, 1) if fin else m
                    T(_mm)
            pr.wait("scalar", sT, gate_mm[NT - 1])
            gex_all = SC(lambda e: e.activation(SM32[:, 0:8 * NT], PSU[:, 0:8 * NT], AF.Exp))
            gex = {mt: gex_all for mt in range(NT)}
            g_done = {}
            for mt in range(NT):
                pr.wait("vector", sS, gex[mt])
                V(lambda e, o=gsum(mt), i=gexp(mt): e.reduce_sum(o, i, AX.X),
                  chain=False)
                V(lambda e, o=grin(mt), i=gsum(mt): e.reciprocal(o, i))
                g_done[mt] = V(lambda e, o=g16(mt), i=gexp(mt), r=grin(mt):
                  e.tensor_scalar(o, i, r, None, ALU.mult))
            gT_ev = {}
            psbf = PSU[:, 2048:3072].bitcast(BF16)
            for mt in range(NT):
                pr.wait("tensor", sV, g_done[mt])
                tv = sT.bump()
                T(lambda t, o=psbf[0:8, 128 * mt: 128 * mt + 128], i=g16(mt):
                  t.transpose(o, i, ident).then_inc(sT.h, 1))
                pr.wait("vector", sT, tv)
                gT_ev[mt] = V(lambda e, o=gTrow[:, 128 * mt: 128 * mt + 128],
                  i=psbf[0:8, 128 * mt: 128 * mt + 128]:
                  e.tensor_copy(o, i), chain=False)
            pr.wait("tensor", sV, gT_ev[NT - 1])
            pr.wait("tensor", sS, gex[NT - 1])
            grep_mm = 0
            for j in range(NSP):
                fin = (j == NSP - 1)
                if fin:
                    grep_mm = sT.bump()
                def _mm(t, o=PSU[:, j * SPW:(j + 1) * SPW], r=gTrow[:, j * SPW:(j + 1) * SPW], fin=fin):
                    m = t.matmul(o, crep, r, start=True, stop=True)
                    return m.then_inc(sT.h, 1) if fin else m
                T(_mm)
            pr.wait("tensor", sWd, ut_l)
            t_mm = 0
            for j in range(NSP):
                for k in range(NCH):
                    fin = (j == NSP - 1 and k == NCH - 1)
                    if fin:
                        t_mm = sT.bump()
                    def _mm(t, o=PSU[:, 2048 + j * SPW: 2048 + (j + 1) * SPW],
                            l=WAr[:, SSo + 128 * k: SSo + 128 * k + 128],
                            r=xnc(k)[:, j * SPW:(j + 1) * SPW], st=(k == 0),
                            sp=(k == NCH - 1), fin=fin):
                        m = t.matmul(o, l, r, start=st, stop=sp)
                        return m.then_inc(sT.h, 1) if fin else m
                    T(_mm)
            pr.wait("vector", sT, grep_mm)
            V(lambda e: e.tensor_copy(grep16, PSU[:, 0:S]))
            pr.wait("vector", sT, t_mm)
            v_t2 = V(lambda e: e.tensor_tensor(T2b, PSU[:, 2048:2048 + S], grep16, ALU.mult))

            pr.wait("tensor", sWd, wt_l)
            pr.wait("tensor", sWd, vfl_l)
            pr.wait("tensor", sV, v_t2)
            ep1 = [0] * NCH
            for oc in range(NCH):
                half = 2048 * (oc % 2)
                if oc >= 2:
                    pr.wait("tensor", sV, ep1[oc - 2])
                fin_l = 0
                for j in range(NSP):
                    for k in range(NCH):
                        T(lambda t, o=PSU[:, half + j * SPW: half + (j + 1) * SPW],
                          l=WAr[:, WA_ + 1024 * k + 128 * oc: WA_ + 1024 * k + 128 * oc + 128],
                          r=xn_rhs(k, j), st=(k == 0): t.matmul(o, l, r, start=st, stop=False))
                    fin_l = sT.bump()
                    T(lambda t, o=PSU[:, half + j * SPW: half + (j + 1) * SPW],
                      l=WAr[:, SSo + 1024 + 128 * oc: SSo + 1024 + 128 * oc + 128],
                      r=T2b[:, j * SPW:(j + 1) * SPW]:
                      t.matmul(o, l, r, start=False, stop=True).then_inc(sT.h, 1))
                pr.wait("vector", sT, fin_l)
                ep1[oc] = V(lambda e, o=res_mixed(oc), i=PSU[:, half: half + S]:
                  e.tensor_tensor(o, o, i, ALU.add))
            slot_free["A"] = (sT, sT.v)
            slot_free["SS"] = (sT, sT.v)

            wq1_l = wload("A", "wq1", 8192)
            down_l = wload("SS", "down", 2048, sub=0)
            kup_l = wload("SS", "kup", 2048, sub=2048)
            wout_l = wload("B", "wout", 8192)
            sp2a = dsy(dscr[0:4].rearrange("c p t -> p c t"),
                       A16f[:, VA // 2: VA // 2 + 4 * S].rearrange("p (c t) -> p c t", t=S),
                       wait=((sV, ep1[3]),))
            sp2b = dsy(dscr[4:8].rearrange("c p t -> p c t"),
                       R32[:, 4 * S: 8 * S].rearrange("p (c t) -> p c t", t=S),
                       wait=((sV, ep1[7]),))
            ln3 = ln_phase(res_mixed, ready=lambda c: [(sV, ep1[c])],
                           psum_free=(ep1[6], ep1[7]), xn_safe=False)
            pr.emit("gpsimd", lambda g, v=sp2a: g.wait_ge(sD.h, v), kind="wait")
            pr.emit("gpsimd", lambda g, v=lpv0: g.wait_ge(sT.h, v), kind="wait")
            pr.emit("gpsimd", lambda g, v=ln3: g.wait_ge(sV.h, v), kind="wait")
            G(lambda g: g.memset(vaug_ones_all, 1.0))

            pads1 = emit_pads(1)
            pr.wait("tensor", sWd, down_l)
            pr.wait("tensor", sV, ln3)
            lat_ev = [0, 0]
            for c in range(2):
                half = 2048 * (c % 2)
                lmm = 0
                for j in range(NSP):
                    for k in range(NCH):
                        fin = (j == NSP - 1 and k == NCH - 1)
                        if fin:
                            lmm = sT.bump()
                        def _mm(t, o=PSU[:, half + j * SPW: half + (j + 1) * SPW],
                                l=WAr[:, SSo + LD * k + 128 * c: SSo + LD * k + 128 * c + 128],
                                r=xn_rhs(k, j), st=(k == 0), sp=(k == NCH - 1), fin=fin):
                            m = t.matmul(o, l, r, start=st, stop=sp)
                            return m.then_inc(sT.h, 1) if fin else m
                        T(_mm)
                pr.wait("scalar", sT, lmm)
                lat_ev[c] = SC(lambda e, o=latc(c), i=PSU[:, half: half + S]:
                  e.activation(o, i, AF.Copy), chain=False)
            vup_l = d16(WAr[:, SSo: SSo + 2048], dW["vup"], wait=((sT, lmm),))
            pr.wait("tensor", sWd, kup_l)
            pr.wait("tensor", sS, lat_ev[1])
            lat_rhs = lambda k, j: latc(k)[:, j * SPW:(j + 1) * SPW]

            def ev_kp1(c, ps):
                pr.wait("scalar", sD, sp2b)
                vv = SC(lambda e, o=kp(c), i=ps: e.activation(o, i, AF.Copy), chain=False)
                return vv

            kmm1, kev1 = proj(kp, lambda k, c: WAr[:, SSo + 2048 + 1024 * k + 128 * c:
                                                   SSo + 2048 + 1024 * k + 128 * c + 128],
                              kup_l, 2, lat_rhs, ev_kp1)
            pr.wait("tensor", sWd, vup_l)
            pr.wait("tensor", sS, kev1[-1])
            vev1 = {}
            vmm1 = 0
            for mt in range(NT):
                bank = (mt % 4) * 1024
                if mt >= 4:
                    pr.wait("tensor", sV, vev1[mt - 4])
                for nh in range(2):
                    for k in range(2):
                        fin = (k == 1)
                        if fin:
                            vmm1 = sT.bump()
                        def _mm(t, o=PSU[:, bank + 512 * nh: bank + 512 * nh + 512],
                                l=latc(k)[:, 128 * mt: 128 * mt + 128],
                                r=WAr[:, SSo + 1024 * k + 512 * nh:
                                       SSo + 1024 * k + 512 * nh + 512],
                                st=(k == 0), sp=fin, fin=fin):
                            m = t.matmul(o, l, r, start=st, stop=sp)
                            return m.then_inc(sT.h, 1) if fin else m
                        T(_mm)
                pr.wait("vector", sT, vmm1)
                vev1[mt] = V(lambda e, o=vaug_evac_out(mt), i=PSU[:, bank: bank + 1024].rearrange(
                    "p (h c) -> p h c", c=64): e.tensor_copy(o, i), chain=False)
            pr.wait("tensor", sWd, wq1_l)
            pr.wait("tensor", sV, vev1[NT - 1])

            def ev_q1(c, ps):
                pr.wait("scalar", sD, sp2a)
                vv = SC(lambda e, o=qp(c), i=ps:
                  e.activation(o, i, AF.Copy, scale=0.125), chain=False)
                return vv

            qmm1, _ = proj(qp, wsl(WA_), wq1_l, NCH, xn_rhs, ev_q1)
            slot_free["A"] = (sT, qmm1[-1])
            v1_l = wload("A", "v1", 8192)

            pr.barrier((sT, sV, sS, sG))
            kept1 = lambda h: list(range(NT))

            def act1(mc):
                j0 = (128 * mc) // SPW
                return (j0, mc - j0 * (SPW // 128))

            ab1 = lambda h, mc: None
            lpv1, yn1, lsc1 = head_loop(1, kept1, act1, ab1, pads1)

            rl1a = dsy(A16f[:, VA // 2: VA // 2 + 4 * S].rearrange("p (c t) -> p c t", t=S),
                       dscr[0:4].rearrange("c p t -> p c t"), wait=((sT, lpv1),))
            rl1b = dsy(R32[:, 4 * S: 8 * S].rearrange("p (c t) -> p c t", t=S),
                       dscr[4:8].rearrange("c p t -> p c t"))

            ep2 = out_proj(wsl(WB_), wout_l, res_mixed, None,
                           ((sD, rl1a), (sD, rl1b)), yn1)
            slot_free["B"] = (sT, sT.v)
            u2_l = wload("B", "u2", 8192)
            slot_free["SS"] = (sT, sT.v)
            u1_l = wload("SS", "u1", 2048, sub=0)
            v2_l = wload("SS", "v2", 2048, sub=2048)

            ln4 = ln_phase(res_mixed, ready=lambda c: [(sV, ep2[c])],
                           psum_free=(ep2[6], ep2[7]))
            ln4_T = sT.v
            b1_l = d16(b1row, dW["b1r"], wait=((sT, ln4_T), (sV, ln4)))
            b2_l = d16(b2row, dW["b2r"])
            for wl in (u1_l, v1_l, u2_l, v2_l, b1_l, b2_l):
                pr.wait("tensor", sWd, wl)
            pr.wait("tensor", sV, ln4)
            prev_span_done = 0
            prev_qev1 = 0
            for j in range(NSP):
                pr.wait("tensor", sS, prev_qev1)
                pev = [0, 0]
                for c in range(2):
                    pmm = 0
                    for k in range(NCH):
                        fin = (k == NCH - 1)
                        if fin:
                            pmm = sT.bump()
                        def _mm(t, o=PSU[:, 512 * c: 512 * c + 512],
                                l=WAr[:, SSo + RANK * k + 128 * c: SSo + RANK * k + 128 * c + 128],
                                r=xn_rhs(k, j), st=(k == 0), sp=fin, fin=fin):
                            m = t.matmul(o, l, r, start=st, stop=sp)
                            return m.then_inc(sT.h, 1) if fin else m
                        T(_mm)
                    pr.wait("scalar", sT, pmm)
                    pev[c] = SC(lambda e, o=Pt[c], i=PSU[:, 512 * c: 512 * c + 512]:
                      e.activation(o, i, AF.Copy), chain=False)
                pr.wait("tensor", sS, pev[1])
                hev = {}
                for g in range(8):
                    half = 2048 * (g % 2)
                    if g >= 2:
                        pr.wait("tensor", sS, hev[g - 2])
                    elif g == 0 and prev_span_done:
                        pr.wait("tensor", sV, prev_span_done)
                    hmm = 0
                    for q in range(4):
                        hc = 4 * g + q
                        for k in range(2):
                            T(lambda t, o=PSU[:, half + 512 * q: half + 512 * q + 512],
                              l=WAr[:, WA_ + DFF * k + 128 * hc: WA_ + DFF * k + 128 * hc + 128],
                              r=Pt[k], st=(k == 0): t.matmul(o, l, r, start=st, stop=False))
                        fin = (q == 3)
                        if fin:
                            hmm = sT.bump()
                        def _mm(t, o=PSU[:, half + 512 * q: half + 512 * q + 512],
                                l=b1row[:, 128 * hc: 128 * hc + 128], fin=fin):
                            m = t.matmul(o, l, cones[:, 0:512], start=False, stop=True)
                            return m.then_inc(sT.h, 1) if fin else m
                        T(_mm)
                    pr.wait("scalar", sT, hmm)
                    ho = (R16[:, 2048 * g: 2048 * (g + 1)] if H_in_qp
                          else A16[:, H_off + 2048 * g: H_off + 2048 * (g + 1)])
                    hev[g] = SC(lambda e, o=ho, i=PSU[:, half: half + 2048]:
                                e.activation(o, i, AF.Gelu_apprx_tanh), chain=False)
                pr.wait("tensor", sS, hev[6])
                pr.wait("tensor", sS, hev[7])
                qev_ = [0, 0]
                for c in range(2):
                    qmm_ = 0
                    for k in range(32):
                        fin = (k == 31)
                        if fin:
                            qmm_ = sT.bump()
                        def _mm(t, o=PSU[:, 512 * c: 512 * c + 512],
                                l=WAr[:, WB_ + RANK * k + 128 * c: WB_ + RANK * k + 128 * c + 128],
                                r=Hch(k), st=(k == 0), sp=fin, fin=fin):
                            m = t.matmul(o, l, r, start=st, stop=sp)
                            return m.then_inc(sT.h, 1) if fin else m
                        T(_mm)
                    pr.wait("scalar", sT, qmm_)
                    qev_[c] = SC(lambda e, o=Qt[c], i=PSU[:, 512 * c: 512 * c + 512]:
                      e.activation(o, i, AF.Copy), chain=False)
                pr.wait("tensor", sS, qev_[1])
                ep_oc = [0] * NCH
                for oc in range(NCH):
                    bank = 1024 + 512 * (oc % 2)
                    if oc >= 2:
                        pr.wait("tensor", sV, ep_oc[oc - 2])
                    for k in range(2):
                        T(lambda t, o=PSU[:, bank: bank + 512],
                          l=WAr[:, SSo + 2048 + 1024 * k + 128 * oc: SSo + 2048 + 1024 * k + 128 * oc + 128],
                          r=Qt[k], st=(k == 0): t.matmul(o, l, r, start=st, stop=False))
                    fo = sT.bump()
                    T(lambda t, o=PSU[:, bank: bank + 512], l=b2row[:, 128 * oc: 128 * oc + 128]:
                      t.matmul(o, l, cones[:, 0:512], start=False, stop=True).then_inc(sT.h, 1))
                    pr.wait("vector", sT, fo)
                    ep_oc[oc] = V(lambda e, o=res_mixed(oc)[:, j * SPW:(j + 1) * SPW], i=PSU[:, bank: bank + 512]:
                      e.tensor_tensor(o, o, i, ALU.add))
                prev_span_done = ep_oc[NCH - 1]
                prev_qev1 = qev_[1]
                prev_psum_free[0] = ep_oc[NCH - 2]
                prev_psum_free[1] = ep_oc[NCH - 1]
                if j == NSP - 1:
                    prev_hch_free = qmm_
            slot_free["A"] = (sT, sT.v)
            slot_free["B"] = (sT, sT.v)
            slot_free["SS"] = (sT, sT.v)

            dyr = dy[s].rearrange("(c p) t -> p c t", p=128)
            tailT = sT.v
            if s + 1 < nseq:
                dxn = dx[s + 1].rearrange("(c p) t -> p c t", p=128)
                nv = d16(R32[:, 0:4 * S].rearrange("p (c t) -> p c t", t=S),
                         dxn[:, 0:4, :], wait=((sT, prev_hch_free),))
                xin_map = {c: (sWd, nv) for c in range(4)}
            for ci in range(4):
                dq = dsy if ci % 2 == 0 else d16
                dq(dyr[:, 4 + ci:5 + ci, :],
                   R32[:, (4 + ci) * S:(5 + ci) * S].rearrange("p (c t) -> p c t", t=S),
                   wait=((sV, ep_oc[4 + ci]),))
                if s + 1 < nseq:
                    lv = dq(R32[:, (4 + ci) * S:(5 + ci) * S], dxn[:, 4 + ci, :],
                            wait=((sT, tailT),))
                    xin_map[4 + ci] = ((sD if ci % 2 == 0 else sWd), lv)
            yA1 = dsy(dyr[:, 0:2, :],
                      A16f[:, VA // 2: VA // 2 + 2 * S].rearrange("p (c t) -> p c t", t=S),
                      wait=((sV, prev_span_done),))
            yA2 = d16(dyr[:, 2:4, :],
                      A16f[:, VA // 2 + 2 * S: VA // 2 + 4 * S].rearrange("p (c t) -> p c t", t=S),
                      wait=((sV, prev_span_done),))
            prev_ystoreA = (yA1, yA2)
            prev_allT = sT.v

        pr.wait("sync", sD, sD.v)
        pr.wait("sync", sWd, sWd.v)
        pr.wait("sync", sP, sP.v)

        ENGSEM = {"gpsimd": sG, "vector": sV, "scalar": sS}

        def replay(e, eng):
            sem = ENGSEM.get(eng)
            cnt = 0
            for fn, kind in pr.ops[eng]:
                if sem is not None and kind in ("op", "opnc"):
                    if cnt and kind == "op":
                        e.wait_ge(sem.h, cnt)
                    r = fn(e)
                    r.then_inc(sem.h, 1)
                    cnt += 1
                else:
                    fn(e)

        @block.sync
        def _(e):
            replay(e, "sync")

        @block.gpsimd
        def _(e):
            replay(e, "gpsimd")

        @block.tensor
        def _(e):
            replay(e, "tensor")

        @block.vector
        def _(e):
            replay(e, "vector")

        @block.scalar
        def _(e):
            replay(e, "scalar")

    return nc


def _chunked(a, nck):
    K, N = a.shape
    assert K == 128 * nck
    return np.ascontiguousarray(a.reshape(nck, 128, N).transpose(1, 0, 2).reshape(128, nck * N))


def prep_weights(i, S):
    NT = S // 128
    f32 = lambda a: np.ascontiguousarray(np.asarray(a, dtype=np.float32))
    out = {}
    out["wq"] = _chunked(f32(i["L0_attn_Wq"]), 8)
    out["wk"] = _chunked(f32(i["L0_attn_Wk"]), 8)
    out["wv"] = _chunked(f32(i["L0_attn_Wv"]), 8)
    out["wo"] = _chunked(f32(i["L0_attn_Wo"]), 8)
    out["wt"] = _chunked(np.ascontiguousarray(f32(i["L0_ffn_weight"]).T), 8)
    out["ut"] = _chunked(np.ascontiguousarray(f32(i["L0_ffn_lora_U"]).reshape(128, 1024).T), 8)
    out["vfl"] = np.ascontiguousarray(f32(i["L0_ffn_lora_V"]).transpose(0, 2, 1).reshape(128, 1024))
    out["gwt"] = _chunked(np.ascontiguousarray(f32(i["L0_ffn_gate_w"]).T), 8)
    out["wq1"] = _chunked(f32(i["L1_attn_Wq"]).reshape(1024, 1024), 8)
    out["down"] = _chunked(f32(i["L1_attn_Wkv_down"]), 8)
    out["kup"] = _chunked(f32(i["L1_attn_Wk_up"]).reshape(256, 1024), 2)
    out["vup"] = _chunked(f32(i["L1_attn_Wv_up"]).reshape(256, 1024), 2)
    out["wout"] = _chunked(f32(i["L1_attn_Wout"]).reshape(1024, 1024), 8)
    out["u1"] = _chunked(f32(i["L1_ffn_U1"]), 8)
    out["v1"] = _chunked(f32(i["L1_ffn_V1"]), 2)
    out["u2"] = _chunked(f32(i["L1_ffn_U2"]), 32)
    out["v2"] = _chunked(f32(i["L1_ffn_V2"]), 2)
    out["bqp"] = np.ascontiguousarray(f32(i["L0_attn_bq"]).reshape(8, 128).T)
    out["bkp"] = np.ascontiguousarray(f32(i["L0_attn_bk"]).reshape(8, 128).T)
    out["bop"] = np.ascontiguousarray(f32(i["L0_attn_bo"]).reshape(8, 128).T)
    out["bvr"] = f32(i["L0_attn_bv"]).reshape(1, 1024)
    out["b1r"] = f32(i["L1_ffn_b1"]).reshape(1, DFF)
    out["b2r"] = f32(i["L1_ffn_b2"]).reshape(1, 1024)
    sl = slopes16()
    out["cb0"] = np.broadcast_to(
        (-(C0 + sl[:, None] * (S - 1))), (16, S)).astype(bfnp)
    cal = np.zeros((128, 16 * NT), np.float32)
    p = np.arange(128, dtype=np.float64)
    for h in range(16):
        for mc in range(NT):
            cal[:, NT * h + mc] = (sl[h] * (128 * mc + p)).astype(np.float32)
    out["calibi"] = cal
    out["cid"] = np.eye(128, dtype=bfnp)
    out["cmaskL"] = (-30000.0 * np.eye(128)).astype(bfnp)
    out["cmaskR"] = (np.arange(128)[None, :] < np.arange(128)[:, None]).astype(bfnp)
    out["conesb"] = np.ones((128, 384), bfnp)
    rep = np.zeros((8, 128), np.float32)
    for e in range(8):
        rep[e, 16 * e:16 * e + 16] = 1.0 / 16.0
    out["crep"] = rep.astype(bfnp)
    out["cones"] = np.ones((1, 512), bfnp)
    cd = np.zeros((128, 128), np.float32)
    cd[:, 0] = 1.0 / 1024.0
    out["conesd"] = cd.astype(bfnp)
    sf = np.zeros((128, 128), np.float32); sf[0, :] = 1.0
    out["selF"] = sf.astype(bfnp)
    sa = np.zeros((128, 128), np.float32); sa[0, 0:64] = 1.0; sa[32, 64:128] = 1.0
    out["selA"] = sa.astype(bfnp)
    sb = np.zeros((128, 128), np.float32); sb[0, 64:128] = 1.0
    out["selB"] = sb.astype(bfnp)
    return out


def kernel(**inputs):
    S = 2048
    nseq = 2
    ncores = 8
    w = prep_weights(inputs, S)
    x = np.asarray(inputs["L0_attn_x"], dtype=np.float32)
    nc = build(S, nseq)
    in_maps = []
    for c in range(ncores):
        m = dict(w)
        m["x"] = np.ascontiguousarray(x[2 * c: 2 * c + 2].transpose(0, 2, 1))
        in_maps.append(m)
    res = run_bass_kernel_spmd(nc, in_maps, core_ids=list(range(ncores)))
    out = np.empty((16, S, D), np.float32)
    for c in range(ncores):
        yv = np.asarray(res.results[c]["y"])
        out[2 * c: 2 * c + 2] = yv.transpose(0, 2, 1)
    return out
